# version 04
# baseline (speedup 1.0000x reference)
import contextlib
import numpy as np
import ml_dtypes
import concourse.bass as bass
import concourse.mybir as mybir
from concourse.bass_utils import run_bass_kernel_spmd

F32 = mybir.dt.float32
BF16 = mybir.dt.bfloat16
AF = mybir.ActivationFunctionType
ALU = mybir.AluOpType
AX = mybir.AxisListType

P = 128
B_, L, H, D, N, KC, R = 4, 4096, 4096, 8192, 16, 4, 256
DH = D // 2
HH = H // 2
NDC = DH // P
NKH = H // P
NKHH = HH // P
NKD = D // P
TC = 512
NTC = L // TC
TH = L // 2
NF = L // 512
EPS_RMS = 1e-6
EPS_LN = 1e-5
ENGINES = ("sync", "scalar", "vector", "gpsimd", "tensor")
C_NTC = NTC
C_SKIP = set()
PAIRS = [[0, 1], [2, 3], [4, 5], [6, 7]]


def build_nc(debug=False, stop_after=None):
    nc = bass.Bass("TRN2", target_bir_lowering=False, debug=False)
    dram = nc.dram_tensor
    spk = "ExternalOutput" if debug else "Internal"

    xT = dram("xT", [H, L], F32, kind="ExternalInput")
    x_res = dram("x_res", [HH, L], F32, kind="ExternalInput")
    w_in_t = dram("w_in_t", [2 * NDC, P, NKH * P], F32, kind="ExternalInput")
    conv_w_t = dram("conv_w_t", [P, NDC * KC], F32, kind="ExternalInput")
    conv_b_t = dram("conv_b_t", [P, NDC], F32, kind="ExternalInput")
    wx_t = dram("wx_t", [P, NKH * 288], F32, kind="ExternalInput")
    wdt_t = dram("wdt_t", [P, 2 * DH], F32, kind="ExternalInput")
    dt_ln_t = dram("dt_ln_t", [P, 2], F32, kind="ExternalInput")
    b_ln_t = dram("b_ln_t", [16, 1], F32, kind="ExternalInput")
    c_ln_t = dram("c_ln_t", [16, 1], F32, kind="ExternalInput")
    dt_bias_t = dram("dt_bias_t", [P, NDC], F32, kind="ExternalInput")
    a_t = dram("a_t", [P, NDC * N], F32, kind="ExternalInput")
    d_param_t = dram("d_param_t", [P, NDC], F32, kind="ExternalInput")
    w_out_t = dram("w_out_t", [NKHH, P, NKD * P], F32, kind="ExternalInput")
    a_flat_t = dram("a_flat_t", [P, NKHH * P], F32, kind="ExternalInput")
    b_flat_t = dram("b_flat_t", [P, HH], F32, kind="ExternalInput")
    wr_t = dram("wr_t", [P, NKHH * 8], F32, kind="ExternalInput")
    br_t = dram("br_t", [8, 1], F32, kind="ExternalInput")
    rrep_t = dram("rrep_t", [8, P], F32, kind="ExternalInput")
    indrep_t = dram("indrep_t", [16, 16 * P], BF16, kind="ExternalInput")
    ones_r_t = dram("ones_r", [1, P], F32, kind="ExternalInput")
    ones_c_t = dram("ones_c", [P, 1], F32, kind="ExternalInput")
    ones16_t = dram("ones16", [16, 1], F32, kind="ExternalInput")
    eps_t = dram("eps_t", [1, 2], F32, kind="ExternalInput")

    out_ht = dram("out_ht", [HH, L], F32, kind="ExternalOutput")

    ub_sp = dram("ub_sp", [DH, L], BF16, kind=spk)
    gate_sp = dram("gate_sp", [DH, L], BF16, kind=spk)
    delta_sp = dram("delta_sp", [DH, L], F32, kind=spk)
    ssm_in = dram("ssm_in", [288, L], F32, kind="Internal")
    ssm_out = dram("ssm_out", [288, L], F32, kind="Internal")
    ssm_dbg = dram("ssm_dbg", [288, L], F32, kind=spk) if debug else None
    mskb_dbg = dram("mskb_dbg", [P, L], F32, kind=spk) if debug else None
    brow_dbg = dram("brow_dbg", [16, L], F32, kind=spk) if debug else None
    crow_dbg = dram("crow_dbg", [16, L], F32, kind=spk) if debug else None
    yag_in = [dram(f"yag_in{i}", [DH, 2 * TC], BF16, kind="Internal")
              for i in range(NTC // 2)]
    yag_dbg = [dram(f"yag_dbg{i}", [DH, 2 * TC], BF16, kind=spk)
               for i in range(NTC // 2)] if debug else None
    yag_out = [dram(f"yag_out{i}", [D, 2 * TC], BF16, kind="Internal")
               for i in range(NTC // 2)]
    wob_sp = dram("wob_sp", [NKHH, P, NKD * P], BF16, kind="Internal")
    r1_sp = dram("r1_sp", [HH, L], F32, kind=spk)
    x2_sp = dram("x2_sp", [HH, L], F32, kind=spk)
    r2_sp = dram("r2_sp", [HH, L], F32, kind="Internal")
    st1_in = dram("st1_in", [2, L], F32, kind="Internal")
    st1_out = dram("st1_out", [2, L], F32, kind="Internal")
    rtr_in = dram("rtr_in", [8, L], F32, kind="Internal")
    rtr_out = dram("rtr_out", [8, L], F32, kind="Internal")
    low_in = dram("low_in", [P, L], F32, kind="Internal")
    low_out = dram("low_out", [P, L], F32, kind="Internal")
    low_dbg = dram("low_dbg", [P, L], F32, kind=spk) if debug else None
    st2_in = dram("st2_in", [2, L], F32, kind="Internal")
    st2_out = dram("st2_out", [2, L], F32, kind="Internal")

    es = contextlib.ExitStack()
    with es:
        E = es.enter_context
        blk = E(nc.Block())
        phz = E(nc.semaphore("phz"))
        dbgg = E(nc.semaphore("dbgg")) if debug else None
        phase_idx = [0]

        def sems(prefix, names):
            return {n: E(nc.semaphore(prefix + n)) for n in names}

        def barrier():
            phase_idx[0] += 1
            target = len(ENGINES) * phase_idx[0]
            for name in ENGINES:
                @getattr(blk, name)
                def _(eng):
                    eng.drain()
                    eng.sem_inc(phz, 1)
                    eng.wait_ge(phz, target)

        conv_w_sb = E(nc.sbuf_tensor("conv_w_sb", [P, NDC * KC], F32))
        conv_b_sb = E(nc.sbuf_tensor("conv_b_sb", [P, NDC], F32))
        dt_bias_sb = E(nc.sbuf_tensor("dt_bias_sb", [P, NDC], F32))
        a_sb = E(nc.sbuf_tensor("a_sb", [P, NDC * N], F32))
        d_param_sb = E(nc.sbuf_tensor("d_param_sb", [P, NDC], F32))
        halo_sb = E(nc.sbuf_tensor("halo_sb", [P, NDC * 3], F32))
        state_sb = E(nc.sbuf_tensor("state_sb", [P, NDC * N], F32))
        indrep_sb = E(nc.sbuf_tensor("indrep_sb", [16, 16 * P], BF16))
        ones_r_sb = E(nc.sbuf_tensor("ones_r_sb", [1, P], F32))
        ones_c_sb = E(nc.sbuf_tensor("ones_c_sb", [P, 1], F32))
        ones16_sb = E(nc.sbuf_tensor("ones16_sb", [16, 1], F32))
        b_ln_sb = E(nc.sbuf_tensor("b_ln_sb", [16, 1], F32))
        c_ln_sb = E(nc.sbuf_tensor("c_ln_sb", [16, 1], F32))
        br_sb = E(nc.sbuf_tensor("br_sb", [8, 1], F32))
        rrep_sb = E(nc.sbuf_tensor("rrep_sb", [8, P], F32))
        dt_ln_sb = E(nc.sbuf_tensor("dt_ln_sb", [P, 2], F32))
        eps_sb = E(nc.sbuf_tensor("eps_sb", [1, 2], F32))

        p0d = E(nc.semaphore("p0d"))
        loads0 = [
            (conv_w_sb, conv_w_t), (conv_b_sb, conv_b_t),
            (dt_bias_sb, dt_bias_t), (a_sb, a_t),
            (d_param_sb, d_param_t), (indrep_sb, indrep_t),
            (ones_r_sb, ones_r_t), (ones_c_sb, ones_c_t),
            (ones16_sb, ones16_t), (b_ln_sb, b_ln_t),
            (c_ln_sb, c_ln_t), (br_sb, br_t), (rrep_sb, rrep_t),
            (dt_ln_sb, dt_ln_t), (eps_sb, eps_t),
        ]

        @blk.sync
        def _(sync):
            for sb, dr in loads0:
                sync.dma_start(out=sb[:, :], in_=dr.ap()).then_inc(p0d, 16)
            sync.wait_ge(p0d, 16 * len(loads0))

        @blk.vector
        def _(v):
            v.memset(halo_sb[:, :], 0.0)
            v.memset(state_sb[:, :], 0.0)
        barrier()

        if stop_after == "P0":
            return nc
        for th in range(2):
            t0 = th * TH
            with contextlib.ExitStack() as ps:
                PS = ps.enter_context
                nm = f"A{th}"
                x_stage = [PS(nc.sbuf_tensor(f"{nm}x_stage{i}", [P, TH // 2], F32)) for i in range(2)]
                x_bf = PS(nc.sbuf_tensor(f"{nm}x_bf", [P, NKH * TH], BF16))
                wstage = [PS(nc.sbuf_tensor(f"{nm}wstage{i}", [P, NKH * P // 2], F32)) for i in range(2)]
                wbf = [PS(nc.sbuf_tensor(f"{nm}wbf{i}", [P, NKH * P], BF16)) for i in range(2)]
                pu = PS(nc.psum_tensor(f"{nm}pu", [P, TH], F32))
                pg = PS(nc.psum_tensor(f"{nm}pg", [P, TH], F32))
                cacc = PS(nc.sbuf_tensor(f"{nm}cacc", [P, TH], F32))
                u_b = [PS(nc.sbuf_tensor(f"{nm}u_b{i}", [P, TH], BF16)) for i in range(2)]
                g_b = [PS(nc.sbuf_tensor(f"{nm}g_b{i}", [P, TH], BF16)) for i in range(2)]
                s = sems(nm, ["ld", "c", "mmu", "mmg",
                              "cv", "uf", "gf", "st"])
                HW = NKH * P // 2

                @blk.sync
                def _(sync):
                    for k in range(NKH):
                        for hv in range(2):
                            j = 2 * k + hv
                            if j >= 2:
                                sync.wait_ge(s["c"], j - 1)
                            sync.wait_ge(s["ld"], 16 * j)
                            sync.dma_start(
                                out=x_stage[j % 2][:, :],
                                in_=xT.ap()[k * P:(k + 1) * P,
                                            t0 + hv * (TH // 2):
                                            t0 + (hv + 1) * (TH // 2)],
                            ).then_inc(s["ld"], 16)
                    for i in range(2 * NDC):
                        dc = i // 2
                        mc = dc if i % 2 == 0 else NDC + dc
                        for hv in range(2):
                            j = 2 * i + hv
                            if j >= 2:
                                sync.wait_ge(s["c"], 2 * NKH + j - 1)
                            sync.wait_ge(s["ld"], 16 * (2 * NKH + j))
                            sync.dma_start(
                                out=wstage[j % 2][:, :],
                                in_=w_in_t.ap()[mc][:, hv * HW:(hv + 1) * HW],
                            ).then_inc(s["ld"], 16)

                @blk.vector
                def _(v):
                    for k in range(NKH):
                        for hv in range(2):
                            j = 2 * k + hv
                            v.wait_ge(s["ld"], 16 * (j + 1))
                            v.tensor_copy(
                                x_bf[:, k * TH + hv * (TH // 2):
                                     k * TH + (hv + 1) * (TH // 2)],
                                x_stage[j % 2][:, :]).then_inc(s["c"], 1)
                    v.drain()

                    def wcast(i):
                        for hv in range(2):
                            j = 2 * i + hv
                            v.wait_ge(s["ld"], 16 * (2 * NKH + j + 1))
                            if i >= 2 and hv == 0:
                                t = i - 2
                                if t % 2 == 0:
                                    v.wait_ge(s["mmu"], t // 2 + 1)
                                else:
                                    v.wait_ge(s["mmg"], t // 2 + 1)
                            v.tensor_copy(wbf[i % 2][:, hv * HW:(hv + 1) * HW],
                                          wstage[j % 2][:, :]).then_inc(s["c"], 1)
                            v.drain()

                    wcast(0)
                    wcast(1)
                    cw = conv_w_sb
                    for dc in range(NDC):
                        v.wait_ge(s["mmu"], dc + 1)
                        if dc >= 1:
                            v.wait_ge(s["uf"], dc)
                        w0 = cw[:, dc * 4 + 0:dc * 4 + 1]
                        w1 = cw[:, dc * 4 + 1:dc * 4 + 2]
                        w2 = cw[:, dc * 4 + 2:dc * 4 + 3]
                        w3 = cw[:, dc * 4 + 3:dc * 4 + 4]
                        hal = halo_sb[:, dc * 3:(dc + 1) * 3]
                        v.tensor_scalar_mul(cacc[:, :], pu[:, :], w3)
                        v.drain()
                        v.scalar_tensor_tensor(cacc[:, 1:TH], pu[:, 0:TH - 1], w2,
                                               cacc[:, 1:TH], ALU.mult, ALU.add)
                        v.drain()
                        v.scalar_tensor_tensor(cacc[:, 0:1], hal[:, 2:3], w2,
                                               cacc[:, 0:1], ALU.mult, ALU.add)
                        v.drain()
                        v.scalar_tensor_tensor(cacc[:, 2:TH], pu[:, 0:TH - 2], w1,
                                               cacc[:, 2:TH], ALU.mult, ALU.add)
                        v.drain()
                        v.scalar_tensor_tensor(cacc[:, 0:2], hal[:, 1:3], w1,
                                               cacc[:, 0:2], ALU.mult, ALU.add)
                        v.drain()
                        v.scalar_tensor_tensor(cacc[:, 3:TH], pu[:, 0:TH - 3], w0,
                                               cacc[:, 3:TH], ALU.mult, ALU.add)
                        v.drain()
                        v.scalar_tensor_tensor(cacc[:, 0:3], hal[:, 0:3], w0,
                                               cacc[:, 0:3], ALU.mult, ALU.add)
                        v.drain()
                        v.tensor_copy(hal, pu[:, TH - 3:TH]).then_inc(s["cv"], 1)
                        v.drain()
                        if 2 * dc + 2 < 2 * NDC:
                            wcast(2 * dc + 2)
                        if 2 * dc + 3 < 2 * NDC:
                            wcast(2 * dc + 3)

                @blk.tensor
                def _(te):
                    for i in range(2 * NDC):
                        dc = i // 2
                        kind = "u" if i % 2 == 0 else "g"
                        te.wait_ge(s["c"], 2 * NKH + 2 * (i + 1))
                        if kind == "u":
                            if dc >= 1:
                                te.wait_ge(s["cv"], dc)
                            pt = pu
                        else:
                            if dc >= 1:
                                te.wait_ge(s["gf"], dc)
                            pt = pg
                        for k in range(NKH):
                            for f in range(TH // 512):
                                ins = te.matmul(
                                    pt[:, f * 512:(f + 1) * 512],
                                    wbf[i % 2][:, k * P:(k + 1) * P],
                                    x_bf[:, k * TH + f * 512:k * TH + (f + 1) * 512],
                                    start=(k == 0), stop=(k == NKH - 1),
                                )
                                pass
                        te.drain()
                        te.sem_inc(s["mmu" if kind == "u" else "mmg"], 1)

                @blk.scalar
                def _(sc):
                    for dc in range(NDC):
                        buf = dc % 2
                        sc.wait_ge(s["cv"], dc + 1)
                        if dc >= 2:
                            sc.wait_ge(s["st"], 16 * 2 * (dc - 1))
                        sc.activation(u_b[buf][:, :], cacc[:, :], AF.Silu,
                                      bias=conv_b_sb[:, dc:dc + 1]).then_inc(s["uf"], 1)
                        sc.wait_ge(s["mmg"], dc + 1)
                        sc.activation(g_b[buf][:, :], pg[:, :], AF.Silu) \
                            .then_inc(s["gf"], 1)
                        sc.drain()

                @blk.gpsimd
                def _(g):
                    for dc in range(NDC):
                        buf = dc % 2
                        g.wait_ge(s["uf"], dc + 1)
                        g.wait_ge(s["st"], 16 * 2 * dc)
                        g.dma_start(out=ub_sp.ap()[dc * P:(dc + 1) * P, t0:t0 + TH],
                                    in_=u_b[buf][:, :]).then_inc(s["st"], 16)
                        g.wait_ge(s["gf"], dc + 1)
                        g.wait_ge(s["st"], 16 * (2 * dc + 1))
                        g.dma_start(out=gate_sp.ap()[dc * P:(dc + 1) * P, t0:t0 + TH],
                                    in_=g_b[buf][:, :]).then_inc(s["st"], 16)
                    g.wait_ge(s["st"], 16 * 2 * NDC)
                barrier()

        if stop_after == "A":
            return nc
        bc_stack = contextlib.ExitStack()
        BCE = bc_stack.enter_context
        brow_sb = BCE(nc.sbuf_tensor("brow_sb", [16, L], BF16))
        crow_sb = BCE(nc.sbuf_tensor("crow_sb", [16, L], BF16))
        b_stack = contextlib.ExitStack()
        BE = b_stack.enter_context
        dt0b_sb = BE(nc.sbuf_tensor("dt0b_sb", [P, L], BF16))
        dt1b_sb = BE(nc.sbuf_tensor("dt1b_sb", [P, L], BF16))
        mskb_bc = BE(nc.sbuf_tensor("mskb_bc", [P, L], F32))

        with contextlib.ExitStack() as ps:
            PS = ps.enter_context
            wx_stage = PS(nc.sbuf_tensor("wx_stage", [P, NKH * 288 // 2], F32))
            wx_bf = PS(nc.sbuf_tensor("wx_bf", [P, NKH * 288], BF16))
            ub_tile = [PS(nc.sbuf_tensor(f"ub_tile{i}", [P, 512], BF16)) for i in range(3)]
            p_dt0 = PS(nc.psum_tensor("p_dt0", [P, 512], F32))
            p_dt1 = PS(nc.psum_tensor("p_dt1", [P, 512], F32))
            p_bcB = PS(nc.psum_tensor("p_bcB", [16, 512], F32))
            p_bcC = PS(nc.psum_tensor("p_bcC", [16, 512], F32))
            t_dt0 = [PS(nc.sbuf_tensor(f"t_dt0{i}", [P, 512], F32)) for i in range(2)]
            t_dt1 = [PS(nc.sbuf_tensor(f"t_dt1{i}", [P, 512], F32)) for i in range(2)]
            t_bcB = [PS(nc.sbuf_tensor(f"t_bcB{i}", [16, 512], F32)) for i in range(2)]
            t_bcC = [PS(nc.sbuf_tensor(f"t_bcC{i}", [16, 512], F32)) for i in range(2)]
            s = sems("B", ["wld", "wc", "uld0", "uld1", "uld2", "uf",
                           "ev2", "shp", "ar"])
            WXH = NKH * 288 // 2

            @blk.sync
            def _(sync):
                for hv in range(2):
                    if hv == 1:
                        sync.wait_ge(s["wc"], 1)
                    sync.dma_start(out=wx_stage[:, :],
                                   in_=wx_t.ap()[:, hv * WXH:(hv + 1) * WXH]) \
                        .then_inc(s["wld"], 16)
                i = 0
                for f in range(NF):
                    for k in range(NKH):
                        if i >= 3:
                            sync.wait_ge(s["uf"], i - 2)
                        sync.dma_start(
                            out=ub_tile[i % 3][:, :],
                            in_=ub_sp.ap()[k * P:(k + 1) * P, f * 512:(f + 1) * 512],
                        ).then_inc(s["uld%d" % (i % 3)], 16)
                        i += 1

            @blk.scalar
            def _(sc):
                for hv in range(2):
                    sc.wait_ge(s["wld"], 16 * (hv + 1))
                    sc.copy(wx_bf[:, hv * WXH:(hv + 1) * WXH], wx_stage[:, :]) \
                        .then_inc(s["wc"], 1)
                    sc.drain()

            @blk.tensor
            def _(te):
                te.wait_ge(s["wc"], 2)
                i = 0
                for f in range(NF):
                    if f >= 1:
                        te.wait_ge(s["ev2"], f)
                    for k in range(NKH):
                        te.wait_ge(s["uld%d" % (i % 3)], 16 * (i // 3 + 1))
                        rhs = ub_tile[i % 3][:, :]
                        st, sp = (k == 0), (k == NKH - 1)
                        te.matmul(p_dt0[:, :], wx_bf[:, k * 288:k * 288 + P], rhs,
                                  start=st, stop=sp)
                        te.matmul(p_dt1[:, :], wx_bf[:, k * 288 + P:k * 288 + 2 * P],
                                  rhs, start=st, stop=sp)
                        te.matmul(p_bcB[:, :], wx_bf[:, k * 288 + 256:k * 288 + 272],
                                  rhs, start=st, stop=sp)
                        te.matmul(p_bcC[:, :],
                                  wx_bf[:, k * 288 + 272:k * 288 + 288],
                                  rhs, start=st, stop=sp)
                        te.drain()
                        te.sem_inc(s["uf"], 1)
                        i += 1

            @blk.scalar
            def _(sc):
                for f in range(NF):
                    par = f % 2
                    sc.wait_ge(s["uf"], NKH * (f + 1))
                    if f >= 2:
                        sc.wait_ge(s["shp"], 16 * 4 * (f - 1))
                    sc.copy(t_dt0[par][:, :], p_dt0[:, :])
                    sc.copy(t_dt1[par][:, :], p_dt1[:, :])
                    sc.copy(t_bcB[par][:, :], p_bcB[:, :])
                    sc.copy(t_bcC[par][:, :], p_bcC[:, :]).then_inc(s["ev2"], 1)
                    sc.drain()

            @blk.gpsimd
            def _(g):
                for f in range(NF):
                    par = f % 2
                    fs = slice(f * 512, (f + 1) * 512)
                    g.wait_ge(s["ev2"], f + 1)
                    g.wait_ge(s["shp"], 64 * f)
                    g.dma_start(out=ssm_in.ap()[0:P, fs], in_=t_dt0[par][:, :]) \
                        .then_inc(s["shp"], 16)
                    g.dma_start(out=ssm_in.ap()[P:2 * P, fs], in_=t_dt1[par][:, :]) \
                        .then_inc(s["shp"], 16)
                    g.dma_start(out=ssm_in.ap()[256:272, fs], in_=t_bcB[par][:, :]) \
                        .then_inc(s["shp"], 16)
                    g.dma_start(out=ssm_in.ap()[272:288, fs], in_=t_bcC[par][:, :]) \
                        .then_inc(s["shp"], 16)
                g.wait_ge(s["shp"], 16 * 4 * NF)
                g.collective_compute(
                    "AllReduce", ALU.add, replica_groups=PAIRS,
                    ins=[ssm_in.ap().opt()], outs=[ssm_out.ap().opt()],
                ).then_inc(s["ar"], 1)
                g.wait_ge(s["ar"], 1)
                if debug:
                    g.dma_start(out=ssm_dbg.ap(), in_=ssm_out.ap()) \
                        .then_inc(dbgg, 16)
                    g.wait_ge(dbgg, 16)
            barrier()

        if stop_after == "B1":
            b_stack.close(); bc_stack.close()
            return nc
        with contextlib.ExitStack() as ps:
            PS = ps.enter_context
            rdt0 = [PS(nc.sbuf_tensor(f"rdt0{i}", [P, 512], F32)) for i in range(2)]
            rdt1 = [PS(nc.sbuf_tensor(f"rdt1{i}", [P, 512], F32)) for i in range(2)]
            rbcB = [PS(nc.sbuf_tensor(f"rbcB{i}", [16, 512], F32)) for i in range(2)]
            rbcC = [PS(nc.sbuf_tensor(f"rbcC{i}", [16, 512], F32)) for i in range(2)]
            qdt = PS(nc.sbuf_tensor("qdt", [P, 512], F32))
            qdt2 = PS(nc.sbuf_tensor("qdt2", [P, 512], F32))
            qbc = PS(nc.sbuf_tensor("qbc", [16, 512], F32))
            qbc2 = PS(nc.sbuf_tensor("qbc2", [16, 512], F32))
            p_st = PS(nc.psum_tensor("p_st", [1, 512], F32))
            p_stB = PS(nc.psum_tensor("p_stB", [1, 512], F32))
            p_stC = PS(nc.psum_tensor("p_stC", [1, 512], F32))
            ss_row = PS(nc.sbuf_tensor("ss_row", [1, L], F32))
            ssB_row = PS(nc.sbuf_tensor("ssB_row", [1, L], F32))
            ssC_row = PS(nc.sbuf_tensor("ssC_row", [1, L], F32))
            p_bc = PS(nc.psum_tensor("p_bc", [P, 512], F32))
            p_bc16 = PS(nc.psum_tensor("p_bc16", [16, 512], F32))
            s = sems("Bb", ["rld", "sq", "smm", "sev", "msk", "bcmm", "bcev",
                            "rld2"])

            @blk.sync
            def _(sync):
                for f in range(NF):
                    par = f % 2
                    fs = slice(f * 512, (f + 1) * 512)
                    if f >= 2:
                        sync.wait_ge(s["sq"], f - 1)
                    sync.wait_ge(s["rld"], 64 * f)
                    sync.dma_start(out=rdt0[par][:, :], in_=ssm_out.ap()[0:P, fs]) \
                        .then_inc(s["rld"], 16)
                    sync.dma_start(out=rdt1[par][:, :], in_=ssm_out.ap()[P:2 * P, fs]) \
                        .then_inc(s["rld"], 16)
                    sync.dma_start(out=rbcB[par][:, :], in_=ssm_out.ap()[256:272, fs]) \
                        .then_inc(s["rld"], 16)
                    sync.dma_start(out=rbcC[par][:, :], in_=ssm_out.ap()[272:288, fs]) \
                        .then_inc(s["rld"], 16)
                sync.wait_ge(s["sq"], NF)
                for f in range(NF):
                    par = f % 2
                    fs = slice(f * 512, (f + 1) * 512)
                    if f >= 2:
                        sync.wait_ge(s["bcev"], 3 * (f - 1))
                    sync.wait_ge(s["rld2"], 32 * f)
                    sync.dma_start(out=rbcB[par][:, :], in_=ssm_out.ap()[256:272, fs]) \
                        .then_inc(s["rld2"], 16)
                    sync.dma_start(out=rbcC[par][:, :], in_=ssm_out.ap()[272:288, fs]) \
                        .then_inc(s["rld2"], 16)

            @blk.scalar
            def _(sc):
                for f in range(NF):
                    par = f % 2
                    fs = slice(f * 512, (f + 1) * 512)
                    sc.wait_ge(s["rld"], 16 * 4 * (f + 1))
                    if f >= 1:
                        sc.wait_ge(s["smm"], f)
                    sc.square(qdt[:, :], rdt0[par][:, :])
                    sc.square(qdt2[:, :], rdt1[par][:, :])
                    sc.square(qbc[:, :], rbcB[par][:, :])
                    sc.square(qbc2[:, :], rbcC[par][:, :])
                    sc.copy(dt0b_sb[:, fs], rdt0[par][:, :])
                    sc.copy(dt1b_sb[:, fs], rdt1[par][:, :]).then_inc(s["sq"], 1)
                    sc.drain()

            @blk.tensor
            def _(te):
                for f in range(NF):
                    te.wait_ge(s["sq"], f + 1)
                    if f >= 1:
                        te.wait_ge(s["sev"], f)
                    te.matmul(p_st[:, :], ones_c_sb[:, :], qdt[:, :],
                              start=True, stop=False)
                    te.matmul(p_st[:, :], ones_c_sb[:, :], qdt2[:, :],
                              start=False, stop=True)
                    te.matmul(p_stB[:, :], ones16_sb[:, :], qbc[:, :],
                              start=True, stop=True)
                    te.matmul(p_stC[:, :], ones16_sb[:, :], qbc2[:, :],
                              start=True, stop=True)
                    te.drain()
                    te.sem_inc(s["smm"], 1)

            @blk.vector
            def _(v):
                for f in range(NF):
                    fs = slice(f * 512, (f + 1) * 512)
                    v.wait_ge(s["smm"], f + 1)
                    v.tensor_copy(ss_row[:, fs], p_st[:, :])
                    v.tensor_copy(ssB_row[:, fs], p_stB[:, :])
                    v.tensor_copy(ssC_row[:, fs], p_stC[:, :]).then_inc(s["sev"], 1)
                v.drain()

            @blk.scalar
            def _(sc):
                sc.wait_ge(s["sev"], NF)
                sc.activation(ss_row[:, :], ss_row[:, :], AF.Ln,
                              scale=1.0 / R, bias=eps_sb[0:1, 0:1])
                sc.drain()
                sc.activation(ss_row[:, :], ss_row[:, :], AF.Exp, scale=-0.5)
                sc.drain()
                sc.activation(ssB_row[:, :], ssB_row[:, :], AF.Ln,
                              scale=1.0 / N, bias=eps_sb[0:1, 0:1])
                sc.drain()
                sc.activation(ssB_row[:, :], ssB_row[:, :], AF.Exp, scale=-0.5)
                sc.drain()
                sc.activation(ssC_row[:, :], ssC_row[:, :], AF.Ln,
                              scale=1.0 / N, bias=eps_sb[0:1, 0:1])
                sc.drain()
                sc.activation(ssC_row[:, :], ssC_row[:, :], AF.Exp, scale=-0.5) \
                    .then_inc(s["msk"], 1)
                sc.drain()

            @blk.tensor
            def _(te):
                te.wait_ge(s["msk"], 1)
                for f in range(NF):
                    if f >= 1:
                        te.wait_ge(s["bcev"], 3 * f)
                    fs = slice(f * 512, (f + 1) * 512)
                    te.matmul(p_bc[:, :], ones_r_sb[:, :], ss_row[:, fs],
                              start=True, stop=True)
                    te.matmul(p_bc16[:, :], ones_r_sb[:, 0:16], ssB_row[:, fs],
                              start=True, stop=True)
                    te.drain()
                    te.sem_inc(s["bcmm"], 1)
                    te.wait_ge(s["bcev"], 3 * f + 2)
                    te.matmul(p_bc16[:, :], ones_r_sb[:, 0:16], ssC_row[:, fs],
                              start=True, stop=True)
                    te.drain()
                    te.sem_inc(s["bcmm"], 1)

            @blk.vector
            def _(v):
                for f in range(NF):
                    par = f % 2
                    fs = slice(f * 512, (f + 1) * 512)
                    v.wait_ge(s["bcmm"], 2 * f + 1)
                    v.tensor_copy(mskb_bc[:, fs], p_bc[:, :]).then_inc(s["bcev"], 1)
                    v.drain()
                    v.wait_ge(s["rld2"], 16 * 2 * (f + 1))
                    v.scalar_tensor_tensor(brow_sb[:, fs], rbcB[par][:, :],
                                           b_ln_sb[:, :], p_bc16[:, :],
                                           ALU.mult, ALU.mult).then_inc(s["bcev"], 1)
                    v.drain()
                    v.wait_ge(s["bcmm"], 2 * f + 2)
                    v.scalar_tensor_tensor(crow_sb[:, fs], rbcC[par][:, :],
                                           c_ln_sb[:, :], p_bc16[:, :],
                                           ALU.mult, ALU.mult).then_inc(s["bcev"], 1)
                    v.drain()

            if debug:
                @blk.gpsimd
                def _(g):
                    g.wait_ge(s["bcev"], 3 * NF)
                    g.dma_start(out=mskb_dbg.ap(), in_=mskb_bc[:, :]).then_inc(dbgg, 16)
                    g.wait_ge(dbgg, 32)
                    g.dma_start(out=brow_dbg.ap(), in_=brow_sb[:, :]).then_inc(dbgg, 16)
                    g.wait_ge(dbgg, 48)
                    g.dma_start(out=crow_dbg.ap(), in_=crow_sb[:, :]).then_inc(dbgg, 16)
                    g.wait_ge(dbgg, 64)
            barrier()

        if stop_after == "B2":
            b_stack.close(); bc_stack.close()
            return nc
        with contextlib.ExitStack() as ps:
            PS = ps.enter_context
            wdt_stage = PS(nc.sbuf_tensor("wdt_stage", [P, DH], F32))
            wdt_bf = PS(nc.sbuf_tensor("wdt_bf", [P, 2 * DH], BF16))
            p_m3 = [PS(nc.psum_tensor(f"p_m3{i}", [P, 512], F32)) for i in range(2)]
            dpre = [PS(nc.sbuf_tensor(f"dpre{i}", [P, 512], F32)) for i in range(2)]
            dexp = [PS(nc.sbuf_tensor(f"dexp{i}", [P, 512], F32)) for i in range(2)]
            dout = [PS(nc.sbuf_tensor(f"dout{i}", [P, 512], F32)) for i in range(2)]
            s = sems("Bc", ["wld", "wc", "mm3", "pre3", "d3", "dst"])

            @blk.sync
            def _(sync):
                for kk in range(2):
                    if kk == 1:
                        sync.wait_ge(s["wc"], 1)
                    sync.dma_start(out=wdt_stage[:, :],
                                   in_=wdt_t.ap()[:, kk * DH:(kk + 1) * DH]) \
                        .then_inc(s["wld"], 16)

            @blk.scalar
            def _(sc):
                for kk in range(2):
                    sc.wait_ge(s["wld"], 16 * (kk + 1))
                    sc.activation(wdt_bf[:, kk * DH:(kk + 1) * DH], wdt_stage[:, :],
                                  AF.Copy, scale=dt_ln_sb[:, kk:kk + 1]) \
                        .then_inc(s["wc"], 1)
                    sc.drain()

            @blk.tensor
            def _(te):
                te.wait_ge(s["wc"], 2)
                it = 0
                for mc in range(NDC):
                    for f in range(NF):
                        par = it % 2
                        if it >= 2:
                            te.wait_ge(s["pre3"], it - 1)
                        fs = slice(f * 512, (f + 1) * 512)
                        te.matmul(p_m3[par][:, :], wdt_bf[:, mc * P:(mc + 1) * P],
                                  dt0b_sb[:, fs], start=True, stop=False)
                        te.matmul(p_m3[par][:, :],
                                  wdt_bf[:, DH + mc * P:DH + (mc + 1) * P],
                                  dt1b_sb[:, fs], start=False, stop=True)
                        te.drain()
                        te.sem_inc(s["mm3"], 1)
                        it += 1

            @blk.vector
            def _(v):
                it = 0
                for mc in range(NDC):
                    for f in range(NF):
                        par = it % 2
                        v.wait_ge(s["mm3"], it + 1)
                        if it >= 2:
                            v.wait_ge(s["d3"], it - 1)
                        fs = slice(f * 512, (f + 1) * 512)
                        v.tensor_mul(dpre[par][:, :], p_m3[par][:, :],
                                     mskb_bc[:, fs]).then_inc(s["pre3"], 1)
                        v.drain()
                        it += 1

            @blk.scalar
            def _(sc):
                it = 0
                for mc in range(NDC):
                    for f in range(NF):
                        par = it % 2
                        sc.wait_ge(s["pre3"], it + 1)
                        if it >= 2:
                            sc.wait_ge(s["dst"], 16 * (it - 1))
                        sc.drain()
                        sc.activation(dexp[par][:, :], dpre[par][:, :], AF.Exp,
                                      bias=dt_bias_sb[:, mc:mc + 1])
                        sc.drain()
                        sc.activation(dout[par][:, :], dexp[par][:, :], AF.Ln,
                                      bias=1.0).then_inc(s["d3"], 1)
                        it += 1

            @blk.gpsimd
            def _(g):
                it = 0
                for mc in range(NDC):
                    for f in range(NF):
                        par = it % 2
                        g.wait_ge(s["d3"], it + 1)
                        g.wait_ge(s["dst"], 16 * it)
                        g.dma_start(
                            out=delta_sp.ap()[mc * P:(mc + 1) * P,
                                              f * 512:(f + 1) * 512],
                            in_=dout[par][:, :]).then_inc(s["dst"], 16)
                        it += 1
                g.wait_ge(s["dst"], 16 * NDC * NF)
            barrier()
        if stop_after == "B3":
            b_stack.close(); bc_stack.close()
            return nc
        b_stack.close()

        with contextlib.ExitStack() as ps:
            PS = ps.enter_context
            wo_stage = [PS(nc.sbuf_tensor(f"wo_stage{i}", [P, NKD * P // 2], F32)) for i in range(2)]
            wo_bfc = [PS(nc.sbuf_tensor(f"wo_bfc{i}", [P, NKD * P], BF16)) for i in range(2)]
            s = sems("W", ["ld", "c", "st"])
            WH = NKD * P // 2

            @blk.sync
            def _(sync):
                for mc in range(NKHH):
                    for hv in range(2):
                        j = 2 * mc + hv
                        if j >= 2:
                            sync.wait_ge(s["c"], j - 1)
                        sync.wait_ge(s["ld"], 16 * j)
                        sync.dma_start(
                            out=wo_stage[j % 2][:, :],
                            in_=w_out_t.ap()[mc][:, hv * WH:(hv + 1) * WH],
                        ).then_inc(s["ld"], 16)

            @blk.scalar
            def _(sc):
                for mc in range(NKHH):
                    for hv in range(2):
                        j = 2 * mc + hv
                        sc.wait_ge(s["ld"], 16 * (j + 1))
                        if mc >= 2 and hv == 0:
                            sc.wait_ge(s["st"], 16 * (mc - 1))
                        sc.copy(wo_bfc[mc % 2][:, hv * WH:(hv + 1) * WH],
                                wo_stage[j % 2][:, :]).then_inc(s["c"], 1)
                        sc.drain()

            @blk.gpsimd
            def _(g):
                for mc in range(NKHH):
                    g.wait_ge(s["c"], 2 * (mc + 1))
                    g.wait_ge(s["st"], 16 * mc)
                    g.dma_start(out=wob_sp.ap()[mc], in_=wo_bfc[mc % 2][:, :]) \
                        .then_inc(s["st"], 16)
                g.wait_ge(s["st"], 16 * NKHH)
            barrier()

        if stop_after == "Wc":
            bc_stack.close()
            return nc
        with contextlib.ExitStack() as ps:
            PS = ps.enter_context
            dA = [PS(nc.sbuf_tensor(f"dA{i}", [P, N * TC], F32)) for i in range(2)]
            bbig = [PS(nc.sbuf_tensor(f"bbig{i}", [P, N * TC], F32)) for i in range(2)]
            bbc = PS(nc.sbuf_tensor("bbc", [P, N * TC], BF16))
            cbc = PS(nc.sbuf_tensor("cbc", [P, N * TC], BF16))
            d_t = [PS(nc.sbuf_tensor(f"d_t{i}", [P, TC], F32)) for i in range(2)]
            u_t = [PS(nc.sbuf_tensor(f"u_t{i}", [P, TC], BF16)) for i in range(3)]
            g_t = [PS(nc.sbuf_tensor(f"g_t{i}", [P, TC], BF16)) for i in range(3)]
            du_t = PS(nc.sbuf_tensor("du_t", [P, TC], F32))
            i16 = PS(nc.sbuf_tensor("i16", [P, N], F32))
            yred = [PS(nc.sbuf_tensor(f"yred{i}", [P, TC], F32)) for i in range(2)]
            y2t = PS(nc.sbuf_tensor("y2t", [P, TC], F32))
            ybf = [PS(nc.sbuf_tensor(f"ybf{i}", [P, TC], BF16)) for i in range(2)]
            p_bc2 = [PS(nc.psum_tensor(f"p_bc2{i}", [P, 512], F32)) for i in range(2)]
            s = sems("C", ["dld", "ug", "dA", "du", "scn", "cm",
                           "red", "assy", "yst", "bcmm", "bcev", "ag"])
            NTCr = C_NTC
            NIT = NTCr * NDC

            @blk.sync
            def _(sync):
                for it in range(NIT):
                    tc, dc = divmod(it, NDC)
                    rs = slice(dc * P, (dc + 1) * P)
                    cs = slice(tc * TC, (tc + 1) * TC)
                    if it >= 2:
                        sync.wait_ge(s["dA"], it - 1)
                        sync.wait_ge(s["du"], it - 1)
                    sync.wait_ge(s["dld"], 16 * it)
                    sync.dma_start(out=d_t[it % 2][:, :],
                                   in_=delta_sp.ap()[rs, cs]).then_inc(s["dld"], 16)
                    if it >= 3:
                        sync.wait_ge(s["assy"], it - 2)
                    sync.wait_ge(s["ug"], 16 * 2 * it)
                    sync.dma_start(out=u_t[it % 3][:, :],
                                   in_=ub_sp.ap()[rs, cs]).then_inc(s["ug"], 16)
                    sync.wait_ge(s["ug"], 16 * (2 * it + 1))
                    sync.dma_start(out=g_t[it % 3][:, :],
                                   in_=gate_sp.ap()[rs, cs]).then_inc(s["ug"], 16)
                    if it >= 2:
                        pit = it - 2
                        ptc, pdc = divmod(pit, NDC)
                        pcs = slice((ptc % 2) * TC, (ptc % 2 + 1) * TC)
                        sync.wait_ge(s["assy"], pit + 1)
                        sync.wait_ge(s["yst"], 16 * pit)
                        sync.dma_start(
                            out=yag_in[ptc // 2].ap()[pdc * P:(pdc + 1) * P, pcs],
                            in_=ybf[pit % 2][:, :]).then_inc(s["yst"], 16)
                for pit in (NIT - 2, NIT - 1):
                    ptc, pdc = divmod(pit, NDC)
                    pcs = slice((ptc % 2) * TC, (ptc % 2 + 1) * TC)
                    sync.wait_ge(s["assy"], pit + 1)
                    sync.wait_ge(s["yst"], 16 * pit)
                    sync.dma_start(out=yag_in[ptc // 2].ap()[pdc * P:(pdc + 1) * P, pcs],
                                   in_=ybf[pit % 2][:, :]).then_inc(s["yst"], 16)
                sync.wait_ge(s["yst"], 16 * NIT)

            @blk.tensor
            def _(te):
                for tc in range(NTCr):
                    if tc >= 1:
                        te.wait_ge(s["cm"], tc * NDC)
                    for n in range(N):
                        par = n % 2
                        base = tc * 2 * N + 2 * n
                        if base >= 2:
                            te.wait_ge(s["bcev"], base)
                        te.matmul(p_bc2[par][:, :],
                                  indrep_sb[:, n * P:(n + 1) * P],
                                  brow_sb[:, tc * TC:(tc + 1) * TC],
                                  start=True, stop=True)
                        te.drain()
                        te.sem_inc(s["bcmm"], 1)
                        if base + 1 >= 2:
                            te.wait_ge(s["bcev"], base)
                        te.matmul(p_bc2[1 - par][:, :],
                                  indrep_sb[:, n * P:(n + 1) * P],
                                  crow_sb[:, tc * TC:(tc + 1) * TC],
                                  start=True, stop=True)
                        te.drain()
                        te.sem_inc(s["bcmm"], 1)

            @blk.scalar
            def _(sc):
                for tc in range(NTCr):
                    for n in range(N):
                        par = n % 2
                        base = tc * 2 * N + 2 * n
                        sc.wait_ge(s["bcmm"], base + 1)
                        sc.copy(bbc[:, n * TC:(n + 1) * TC], p_bc2[par][:, :]) \
                            .then_inc(s["bcev"], 1)
                        sc.wait_ge(s["bcmm"], base + 2)
                        sc.copy(cbc[:, n * TC:(n + 1) * TC], p_bc2[1 - par][:, :]) \
                            .then_inc(s["bcev"], 1)
                        sc.drain()
                    for dc in range(NDC):
                        it = tc * NDC + dc
                        slot = it % 2
                        sc.wait_ge(s["dld"], 16 * (it + 1))
                        if it >= 2:
                            sc.wait_ge(s["red"], it - 1)
                        sc.drain()
                        if "dA" in C_SKIP:
                            for n in range(N):
                                ins = sc.copy(dA[slot][:, n * TC:(n + 1) * TC],
                                              d_t[it % 2][:, :])
                        else:
                            for n in range(N):
                                ins = sc.activation(
                                    dA[slot][:, n * TC:(n + 1) * TC],
                                    d_t[it % 2][:, :], AF.Exp,
                                    scale=a_sb[:, dc * N + n:dc * N + n + 1])
                        ins.then_inc(s["dA"], 1)

            @blk.vector
            def _(v):
                def reduce_step(j):
                    v.wait_ge(s["cm"], j + 1)
                    if "reduce" in C_SKIP:
                        v.tensor_copy(yred[j % 2][:, :], dA[j % 2][:, 0:TC]) \
                            .then_inc(s["red"], 1)
                    else:
                        v.tensor_reduce(
                            yred[j % 2][:, :],
                            dA[j % 2][:, :].rearrange("p (n t) -> p t n", n=N),
                            axis=AX.X, op=ALU.add).then_inc(s["red"], 1)
                    v.drain()

                def assy_step(pit):
                    pdc = pit % NDC
                    if pit >= 2:
                        v.wait_ge(s["yst"], 16 * (pit - 1))
                    if "assy" in C_SKIP:
                        v.tensor_copy(y2t[:, :], yred[pit % 2][:, :])
                        v.drain()
                        v.tensor_copy(ybf[pit % 2][:, :], y2t[:, :]) \
                            .then_inc(s["assy"], 1)
                        v.drain()
                        return
                    v.scalar_tensor_tensor(
                        y2t[:, :], u_t[pit % 3][:, :],
                        d_param_sb[:, pdc:pdc + 1], yred[pit % 2][:, :],
                        ALU.mult, ALU.add)
                    v.drain()
                    v.wait_ge(s["ug"], 16 * (2 * pit + 2))
                    v.tensor_mul(ybf[pit % 2][:, :], y2t[:, :],
                                 g_t[pit % 3][:, :]).then_inc(s["assy"], 1)
                    v.drain()

                for it in range(NIT):
                    tc, dc = divmod(it, NDC)
                    slot = it % 2
                    dA3 = dA[slot][:, :].rearrange("p (n t) -> p n t", n=N)
                    bb = bbig[slot]
                    b3 = bb[:, :].rearrange("p (n t) -> p n t", n=N)
                    stt = state_sb[:, dc * N:(dc + 1) * N]
                    v.wait_ge(s["dld"], 16 * (it + 1))
                    v.wait_ge(s["ug"], 16 * (2 * it + 1))
                    v.tensor_mul(du_t[:, :], d_t[it % 2][:, :], u_t[it % 3][:, :]) \
                        .then_inc(s["du"], 1)
                    v.drain()
                    if dc == 0:
                        v.wait_ge(s["bcev"], (tc + 1) * 2 * N)
                    if it >= 2:
                        v.wait_ge(s["cm"], it - 1)
                    if "bmul" in C_SKIP:
                        v.tensor_copy(bb[:, :], bbc[:, :])
                    else:
                        du_b = du_t[:, :].rearrange("p (o t) -> p o t", o=1) \
                            .broadcast_to([P, N, TC])
                        v.tensor_tensor(b3, du_b, bbc[:, :].rearrange(
                            "p (n t) -> p n t", n=N), ALU.mult)
                    v.drain()
                    v.wait_ge(s["dA"], it + 1)
                    if "inject" not in C_SKIP:
                        v.tensor_mul(i16[:, :], dA3[:, :, 0], stt)
                        v.drain()
                        v.tensor_add(b3[:, :, 0], b3[:, :, 0], i16[:, :])
                        v.drain()
                        v.memset(dA3[:, :, 0], 0.0)
                        v.drain()
                    if "scan" in C_SKIP:
                        v.tensor_copy(bb[:, :], dA[slot][:, :])
                    else:
                        v.tensor_tensor_scan(bb[:, :], dA[slot][:, :], bb[:, :],
                                             0.0, ALU.mult, ALU.add)
                    v.drain()
                    v.tensor_copy(stt, b3[:, :, TC - 1]).then_inc(s["scn"], 1)
                    v.drain()
                    if it >= 2:
                        assy_step(it - 2)
                    if it >= 1:
                        reduce_step(it - 1)
                assy_step(NIT - 2)
                reduce_step(NIT - 1)
                assy_step(NIT - 1)

            @blk.gpsimd
            def _(g):
                for it in range(NIT):
                    slot = it % 2
                    g.wait_ge(s["scn"], it + 1)
                    if "cmul" in C_SKIP:
                        g.tensor_copy(dA[slot][:, :], bbig[slot][:, :]) \
                            .then_inc(s["cm"], 1)
                    else:
                        g.tensor_tensor(dA[slot][:, :], bbig[slot][:, :], cbc[:, :],
                                        ALU.mult).then_inc(s["cm"], 1)
                    tc, dc = divmod(it, NDC)
                    if dc == NDC - 1 and tc % 2 == 1:
                        g.wait_ge(s["yst"], 16 * NDC * (tc + 1))
                        if debug:
                            g.wait_ge(dbgg, 64 + 16 * (tc // 2))
                            g.dma_start(out=yag_dbg[tc // 2].ap(),
                                        in_=yag_in[tc // 2].ap()) \
                                .then_inc(dbgg, 16)
                        g.collective_compute(
                            "AllGather", ALU.bypass, replica_groups=PAIRS,
                            ins=[yag_in[tc // 2].ap().opt()],
                            outs=[yag_out[tc // 2].ap().opt()],
                        ).then_inc(s["ag"], 1)
                g.wait_ge(s["ag"], NTCr // 2)
                if debug:
                    g.wait_ge(dbgg, 64 + 16 * (NTCr // 2))
            barrier()
        bc_stack.close()

        if stop_after == "C":
            return nc
        with contextlib.ExitStack() as ps:
            PS = ps.enter_context
            y_slab = PS(nc.sbuf_tensor("y_slab", [P, NKD * 1024], BF16))
            wo_bf = [PS(nc.sbuf_tensor(f"wo_bfd{i}", [P, NKD * P], BF16)) for i in range(2)]
            xres_t = [PS(nc.sbuf_tensor(f"xres_t{i}", [P, 1024], F32)) for i in range(2)]
            r1_t = [PS(nc.sbuf_tensor(f"r1_t{i}", [P, 1024], F32)) for i in range(2)]
            sq_t = [PS(nc.sbuf_tensor(f"sq_td{i}", [P, 1024], F32)) for i in range(2)]
            p_m4 = [PS(nc.psum_tensor(f"p_m4{i}", [P, 1024], F32)) for i in range(2)]
            p_s1 = PS(nc.psum_tensor("p_s1", [1, 1024], F32))
            p_s2 = PS(nc.psum_tensor("p_s2", [1, 1024], F32))
            s1_t = PS(nc.sbuf_tensor("s1_t", [1, 1024], F32))
            s2_t = PS(nc.sbuf_tensor("s2_t", [1, 1024], F32))
            s = sems("D", ["yld", "wx", "mm4", "ev4", "sq", "smm",
                           "ss", "ssv", "r1st", "ar"])
            NSL = 4

            @blk.sync
            def _(sync):
                for sl in range(NSL):
                    for k in range(NKD):
                        if sl >= 1 and k == 0:
                            sync.wait_ge(s["mm4"], sl * NKHH)
                        sync.dma_start(
                            out=y_slab[:, k * 1024:(k + 1) * 1024],
                            in_=yag_out[sl].ap()[k * P:(k + 1) * P, :],
                        ).then_inc(s["yld"], 16)
                    for mc in range(NKHH):
                        i = sl * NKHH + mc
                        if i >= 2:
                            sync.wait_ge(s["mm4"], i - 1)
                        sync.wait_ge(s["wx"], 16 * 2 * i)
                        sync.dma_start(out=wo_bf[i % 2][:, :],
                                       in_=wob_sp.ap()[mc]).then_inc(s["wx"], 16)
                        if i >= 2:
                            sync.wait_ge(s["ev4"], i - 1)
                        sync.wait_ge(s["wx"], 16 * (2 * i + 1))
                        sync.dma_start(
                            out=xres_t[i % 2][:, :],
                            in_=x_res.ap()[mc * P:(mc + 1) * P,
                                           sl * 1024:(sl + 1) * 1024],
                        ).then_inc(s["wx"], 16)

            @blk.tensor
            def _(te):
                for sl in range(NSL):
                    te.wait_ge(s["yld"], 16 * NKD * (sl + 1))
                    for mc in range(NKHH):
                        i = sl * NKHH + mc
                        par = i % 2
                        te.wait_ge(s["wx"], 16 * (2 * i + 1))
                        if i >= 2:
                            te.wait_ge(s["ev4"], i - 1)
                        for k in range(NKD):
                            for f in range(2):
                                ins = te.matmul(
                                    p_m4[par][:, f * 512:(f + 1) * 512],
                                    wo_bf[par][:, k * P:(k + 1) * P],
                                    y_slab[:, k * 1024 + f * 512:
                                           k * 1024 + (f + 1) * 512],
                                    start=(k == 0), stop=(k == NKD - 1))
                        te.drain()
                        te.sem_inc(s["mm4"], 1)
                        if i >= 1:
                            j = i - 1
                            te.wait_ge(s["sq"], j + 1)
                            for f in range(2):
                                fs = slice(f * 512, (f + 1) * 512)
                                te.matmul(p_s1[:, fs], ones_c_sb[:, :],
                                          r1_t[j % 2][:, fs],
                                          start=(j % NKHH == 0),
                                          stop=(j % NKHH == NKHH - 1))
                                te.matmul(p_s2[:, fs], ones_c_sb[:, :],
                                          sq_t[j % 2][:, fs],
                                          start=(j % NKHH == 0),
                                          stop=(j % NKHH == NKHH - 1))
                            te.drain()
                            te.sem_inc(s["smm"], 1)
                j = NSL * NKHH - 1
                te.wait_ge(s["sq"], j + 1)
                for f in range(2):
                    fs = slice(f * 512, (f + 1) * 512)
                    te.matmul(p_s1[:, fs], ones_c_sb[:, :], r1_t[j % 2][:, fs],
                              start=(j % NKHH == 0), stop=True)
                    te.matmul(p_s2[:, fs], ones_c_sb[:, :], sq_t[j % 2][:, fs],
                              start=(j % NKHH == 0), stop=True)
                te.drain()
                te.sem_inc(s["smm"], 1)

            @blk.vector
            def _(v):
                for i in range(NSL * NKHH):
                    par = i % 2
                    v.wait_ge(s["mm4"], i + 1)
                    v.wait_ge(s["wx"], 16 * (2 * i + 2))
                    if i >= 2:
                        v.wait_ge(s["r1st"], 16 * (i - 1))
                        v.wait_ge(s["smm"], i - 1)
                    v.tensor_add(r1_t[par][:, :], p_m4[par][:, :], xres_t[par][:, :]) \
                        .then_inc(s["ev4"], 1)
                    v.drain()
                    v.tensor_mul(sq_t[par][:, :], r1_t[par][:, :], r1_t[par][:, :]) \
                        .then_inc(s["sq"], 1)
                    v.drain()
                    if i % NKHH == NKHH - 1:
                        sl = i // NKHH
                        v.wait_ge(s["smm"], i + 1)
                        if sl >= 1:
                            v.wait_ge(s["ss"], 32 * sl)
                        v.tensor_copy(s1_t[:, :], p_s1[:, :])
                        v.tensor_copy(s2_t[:, :], p_s2[:, :]).then_inc(s["ssv"], 1)
                        v.drain()

            @blk.gpsimd
            def _(g):
                for i in range(NSL * NKHH):
                    sl, mc = divmod(i, NKHH)
                    g.wait_ge(s["ev4"], i + 1)
                    g.wait_ge(s["r1st"], 16 * i)
                    g.dma_start(
                        out=r1_sp.ap()[mc * P:(mc + 1) * P, sl * 1024:(sl + 1) * 1024],
                        in_=r1_t[i % 2][:, :]).then_inc(s["r1st"], 16)
                    if mc == NKHH - 1:
                        g.wait_ge(s["ssv"], sl + 1)
                        g.wait_ge(s["ss"], 32 * sl)
                        g.dma_start(out=st1_in.ap()[0:1, sl * 1024:(sl + 1) * 1024],
                                    in_=s1_t[:, :]).then_inc(s["ss"], 16)
                        g.dma_start(out=st1_in.ap()[1:2, sl * 1024:(sl + 1) * 1024],
                                    in_=s2_t[:, :]).then_inc(s["ss"], 16)
                g.wait_ge(s["r1st"], 16 * NSL * NKHH)
                g.wait_ge(s["ss"], 32 * NSL)
                g.collective_compute(
                    "AllReduce", ALU.add, replica_groups=PAIRS,
                    ins=[st1_in.ap().opt()], outs=[st1_out.ap().opt()],
                ).then_inc(s["ar"], 1)
                g.wait_ge(s["ar"], 1)
            barrier()

        if stop_after == "D1":
            return nc
        x2_stack = contextlib.ExitStack()
        x2b_sb = x2_stack.enter_context(nc.sbuf_tensor("x2b_sb", [P, NKHH * L], BF16))

        with contextlib.ExitStack() as ps:
            PS = ps.enter_context
            s1_row = PS(nc.sbuf_tensor("s1b_row", [1, L], F32))
            s2_row = PS(nc.sbuf_tensor("s2b_row", [1, L], F32))
            tmp512 = PS(nc.sbuf_tensor("tmp512", [1, 512], F32))
            p_bcd = [PS(nc.psum_tensor(f"p_bcd{i}", [P, 512], F32)) for i in range(2)]
            r1d_t = [PS(nc.sbuf_tensor(f"r1d_t{i}", [P, 512], F32)) for i in range(2)]
            x2_t = [PS(nc.sbuf_tensor(f"x2d_t{i}", [P, 512], F32)) for i in range(2)]
            s = sems("E", ["ld", "vi", "bcmm", "rld", "x2", "x2c", "x2st"])

            @blk.sync
            def _(sync):
                sync.dma_start(out=s1_row[:, :], in_=st1_out.ap()[0:1, :]).then_inc(s["ld"], 16)
                sync.dma_start(out=s2_row[:, :], in_=st1_out.ap()[1:2, :]).then_inc(s["ld"], 16)

            @blk.vector
            def _(v):
                v.wait_ge(s["ld"], 32)
                v.tensor_scalar_mul(s1_row[:, :], s1_row[:, :], 1.0 / H)
                v.drain()
                for f in range(NF):
                    fs = slice(f * 512, (f + 1) * 512)
                    v.tensor_mul(tmp512[:, :], s1_row[:, fs], s1_row[:, fs])
                    v.drain()
                    ins = v.scalar_tensor_tensor(s2_row[:, fs], s2_row[:, fs],
                                                 1.0 / H, tmp512[:, :],
                                                 ALU.mult, ALU.subtract)
                    if f == NF - 1:
                        ins.then_inc(s["vi"], 1)
                    v.drain()

            @blk.scalar
            def _(sc):
                sc.wait_ge(s["vi"], 1)
                sc.activation(s2_row[:, :], s2_row[:, :], AF.Ln, bias=eps_sb[0:1, 1:2])
                sc.drain()
                sc.activation(s2_row[:, :], s2_row[:, :], AF.Exp, scale=-0.5) \
                    .then_inc(s["vi"], 1)
                sc.drain()

            @blk.tensor
            def _(te):
                te.wait_ge(s["vi"], 2)
                for f in range(NF):
                    if f >= 1:
                        te.wait_ge(s["x2"], NKHH * f)
                    fs = slice(f * 512, (f + 1) * 512)
                    te.matmul(p_bcd[0][:, :], ones_r_sb[:, :], s1_row[:, fs],
                              start=True, stop=True)
                    te.matmul(p_bcd[1][:, :], ones_r_sb[:, :], s2_row[:, fs],
                              start=True, stop=True)
                    te.drain()
                    te.sem_inc(s["bcmm"], 1)

            @blk.sync
            def _(sync):
                i = 0
                for f in range(NF):
                    for mc in range(NKHH):
                        if i >= 2:
                            sync.wait_ge(s["x2"], i - 1)
                        sync.wait_ge(s["rld"], 16 * i)
                        sync.dma_start(
                            out=r1d_t[i % 2][:, :],
                            in_=r1_sp.ap()[mc * P:(mc + 1) * P, f * 512:(f + 1) * 512],
                        ).then_inc(s["rld"], 16)
                        i += 1

            @blk.vector
            def _(v):
                i = 0
                for f in range(NF):
                    v.wait_ge(s["bcmm"], f + 1)
                    for mc in range(NKHH):
                        par = i % 2
                        v.wait_ge(s["rld"], 16 * (i + 1))
                        if i >= 2:
                            v.wait_ge(s["x2c"], i - 1)
                            v.wait_ge(s["x2st"], 16 * (i - 1))
                        v.tensor_sub(x2_t[par][:, :], r1d_t[par][:, :],
                                     p_bcd[0][:, :])
                        v.drain()
                        v.tensor_mul(x2_t[par][:, :], x2_t[par][:, :],
                                     p_bcd[1][:, :]).then_inc(s["x2"], 1)
                        v.drain()
                        i += 1

            @blk.scalar
            def _(sc):
                i = 0
                for f in range(NF):
                    for mc in range(NKHH):
                        sc.wait_ge(s["x2"], i + 1)
                        sc.copy(x2b_sb[:, mc * L + f * 512:mc * L + (f + 1) * 512],
                                x2_t[i % 2][:, :]).then_inc(s["x2c"], 1)
                        sc.drain()
                        i += 1

            @blk.gpsimd
            def _(g):
                i = 0
                for f in range(NF):
                    for mc in range(NKHH):
                        g.wait_ge(s["x2"], i + 1)
                        g.wait_ge(s["x2st"], 16 * i)
                        g.dma_start(
                            out=x2_sp.ap()[mc * P:(mc + 1) * P, f * 512:(f + 1) * 512],
                            in_=x2_t[i % 2][:, :]).then_inc(s["x2st"], 16)
                        i += 1
                g.wait_ge(s["x2st"], 16 * NKHH * NF)
            barrier()

        if stop_after == "D2":
            x2_stack.close()
            return nc
        e_stack = contextlib.ExitStack()
        EE = e_stack.enter_context
        bfl_bf = EE(nc.sbuf_tensor("bfl_bf", [P, HH], BF16))
        lows_sb = EE(nc.sbuf_tensor("lows_sb", [P, L], BF16))

        with contextlib.ExitStack() as ps:
            PS = ps.enter_context
            s = sems("F", ["ld", "wc", "mmr", "evr", "shp", "ar",
                           "exp", "esmm", "esev", "rec", "repmm", "lows", "lld"])
            wr_bf = PS(nc.sbuf_tensor("wr_bf", [P, NKHH * 8], BF16))
            af_bf = PS(nc.sbuf_tensor("af_bf", [P, NKHH * P], BF16))
            with contextlib.ExitStack() as stg:
                SS = stg.enter_context
                wr_stage = SS(nc.sbuf_tensor("wr_stage", [P, NKHH * 8], F32))
                af_stage = SS(nc.sbuf_tensor("af_stage", [P, NKHH * P], F32))
                bfl_stage = SS(nc.sbuf_tensor("bfl_stage", [P, HH], F32))

                @blk.sync
                def _(sync):
                    sync.dma_start(out=wr_stage[:, :], in_=wr_t.ap()).then_inc(s["ld"], 16)
                    sync.dma_start(out=af_stage[:, :], in_=a_flat_t.ap()).then_inc(s["ld"], 16)
                    sync.dma_start(out=bfl_stage[:, :], in_=b_flat_t.ap()).then_inc(s["ld"], 16)

                @blk.scalar
                def _(sc):
                    sc.wait_ge(s["ld"], 48)
                    sc.copy(wr_bf[:, :], wr_stage[:, :])
                    sc.copy(af_bf[:, :], af_stage[:, :])
                    sc.activation(bfl_bf[:, :], bfl_stage[:, :], AF.Copy, scale=2.0) \
                        .then_inc(s["wc"], 1)

            p_r = [PS(nc.psum_tensor(f"p_r{i}", [8, 512], F32)) for i in range(2)]
            p_low = [PS(nc.psum_tensor(f"p_low{i}", [P, 512], F32)) for i in range(2)]
            t_r = [PS(nc.sbuf_tensor(f"t_r{i}", [8, 512], F32)) for i in range(2)]
            t_low = [PS(nc.sbuf_tensor(f"t_low{i}", [P, 512], F32)) for i in range(2)]
            e_sb = PS(nc.sbuf_tensor("e_sb", [8, L], F32))
            p_es = PS(nc.psum_tensor("p_es", [1, 512], F32))
            es_row = PS(nc.sbuf_tensor("es_row", [1, L], F32))
            rc_row = es_row
            t_er = PS(nc.sbuf_tensor("t_er", [P, 512], F32))
            rlow_t = [PS(nc.sbuf_tensor(f"rlow_t{i}", [P, 512], F32)) for i in range(2)]

            @blk.tensor
            def _(te):
                te.wait_ge(s["wc"], 1)
                for f in range(NF):
                    par = f % 2
                    if f >= 2:
                        te.wait_ge(s["evr"], f - 1)
                    for k in range(NKHH):
                        rhs = x2b_sb[:, k * L + f * 512:k * L + (f + 1) * 512]
                        st, sp = (k == 0), (k == NKHH - 1)
                        te.matmul(p_r[par][:, :], wr_bf[:, k * 8:(k + 1) * 8], rhs,
                                  start=st, stop=sp)
                        te.matmul(p_low[par][:, :],
                                  af_bf[:, k * P:(k + 1) * P], rhs,
                                  start=st, stop=sp)
                    te.drain()
                    te.sem_inc(s["mmr"], 1)

            @blk.vector
            def _(v):
                for f in range(NF):
                    par = f % 2
                    v.wait_ge(s["mmr"], f + 1)
                    if f >= 2:
                        v.wait_ge(s["shp"], 32 * (f - 1))
                    v.tensor_copy(t_r[par][:, :], p_r[par][:, :])
                    v.tensor_copy(t_low[par][:, :], p_low[par][:, :]).then_inc(s["evr"], 1)
                v.drain()

            @blk.gpsimd
            def _(g):
                for f in range(NF):
                    par = f % 2
                    fs = slice(f * 512, (f + 1) * 512)
                    g.wait_ge(s["evr"], f + 1)
                    g.dma_start(out=rtr_in.ap()[:, fs], in_=t_r[par][:, :]) \
                        .then_inc(s["shp"], 16)
                    g.dma_start(out=low_in.ap()[:, fs], in_=t_low[par][:, :]) \
                        .then_inc(s["shp"], 16)
                g.wait_ge(s["shp"], 32 * NF)
                g.collective_compute(
                    "AllReduce", ALU.add, replica_groups=PAIRS,
                    ins=[rtr_in.ap().opt()], outs=[rtr_out.ap().opt()],
                ).then_inc(s["ar"], 1)
                g.collective_compute(
                    "AllReduce", ALU.add, replica_groups=PAIRS,
                    ins=[low_in.ap().opt()], outs=[low_out.ap().opt()],
                ).then_inc(s["ar"], 1)
                if debug:
                    g.wait_ge(s["ar"], 2)
                    g.dma_start(out=low_dbg.ap(), in_=low_out.ap()) \
                        .then_inc(dbgg, 16)
                    g.wait_ge(dbgg, 64 + 16 * (NTC // 2) + 16)

            @blk.sync
            def _(sync):
                sync.wait_ge(s["ar"], 2)
                sync.dma_start(out=e_sb[:, :], in_=rtr_out.ap()).then_inc(s["ld"], 16)

            @blk.scalar
            def _(sc):
                sc.wait_ge(s["ld"], 64)
                sc.activation(e_sb[:, :], e_sb[:, :], AF.Exp, bias=br_sb[:, :]) \
                    .then_inc(s["exp"], 1)
                sc.drain()

            @blk.tensor
            def _(te):
                te.wait_ge(s["exp"], 1)
                for f in range(NF):
                    if f >= 1:
                        te.wait_ge(s["esev"], f)
                    fs = slice(f * 512, (f + 1) * 512)
                    te.matmul(p_es[:, :], ones16_sb[0:8, :], e_sb[:, fs],
                              start=True, stop=True)
                    te.drain()
                    te.sem_inc(s["esmm"], 1)

            @blk.vector
            def _(v):
                for f in range(NF):
                    fs = slice(f * 512, (f + 1) * 512)
                    v.wait_ge(s["esmm"], f + 1)
                    v.tensor_copy(es_row[:, fs], p_es[:, :]).then_inc(s["esev"], 1)
                v.drain()
                v.reciprocal(rc_row[:, :], es_row[:, :]).then_inc(s["rec"], 1)
                v.drain()

            p_rc = [PS(nc.psum_tensor(f"p_rc{i}", [P, 512], F32)) for i in range(2)]

            @blk.tensor
            def _(te):
                te.wait_ge(s["rec"], 1)
                for f in range(NF):
                    par = f % 2
                    if f >= 2:
                        te.wait_ge(s["lows"], f - 1)
                    fs = slice(f * 512, (f + 1) * 512)
                    te.matmul(p_low[par][:, :], rrep_sb[:, :], e_sb[:, fs],
                              start=True, stop=True)
                    te.matmul(p_rc[par][:, :], ones_r_sb[:, :], rc_row[:, fs],
                              start=True, stop=True)
                    te.drain()
                    te.sem_inc(s["repmm"], 1)

            @blk.sync
            def _(sync):
                i = 0
                for f in range(NF):
                    par = f % 2
                    fs = slice(f * 512, (f + 1) * 512)
                    if f >= 2:
                        sync.wait_ge(s["lows"], f - 1)
                    sync.wait_ge(s["lld"], 16 * f)
                    sync.dma_start(out=rlow_t[par][:, :], in_=low_out.ap()[:, fs]) \
                        .then_inc(s["lld"], 16)
                    i += 1

            @blk.vector
            def _(v):
                for f in range(NF):
                    par = f % 2
                    fs = slice(f * 512, (f + 1) * 512)
                    v.wait_ge(s["repmm"], f + 1)
                    v.tensor_copy(t_er[:, :], p_low[par][:, :])
                    v.drain()
                    v.tensor_mul(t_er[:, :], t_er[:, :], p_rc[par][:, :])
                    v.drain()
                    v.wait_ge(s["lld"], 16 * (f + 1))
                    v.tensor_mul(lows_sb[:, fs], rlow_t[par][:, :], t_er[:, :]) \
                        .then_inc(s["lows"], 1)
                    v.drain()
            barrier()

        if stop_after == "E1":
            e_stack.close(); x2_stack.close()
            return nc
        with contextlib.ExitStack() as ps2:
            PS2 = ps2.enter_context
            x2e_t = [PS2(nc.sbuf_tensor(f"x2e_t{i}", [P, 512], F32)) for i in range(2)]
            r2_t = [PS2(nc.sbuf_tensor(f"r2_t{i}", [P, 512], F32)) for i in range(2)]
            sq_t = [PS2(nc.sbuf_tensor(f"sq_te{i}", [P, 512], F32)) for i in range(2)]
            p_f2 = [PS2(nc.psum_tensor(f"p_f2{i}", [P, 512], F32)) for i in range(2)]
            p_s1e = PS2(nc.psum_tensor("p_s1e", [1, 512], F32))
            p_s2e = PS2(nc.psum_tensor("p_s2e", [1, 512], F32))
            s1e_t = PS2(nc.sbuf_tensor("s1e_t", [1, 512], F32))
            s2e_t = PS2(nc.sbuf_tensor("s2e_t", [1, 512], F32))
            se = sems("G", ["xld", "mmf", "ev", "sq", "smm", "ss",
                            "ssv", "r2st", "ar"])

            @blk.sync
            def _(sync):
                i = 0
                for f in range(NF):
                    for mc in range(NKHH):
                        if i >= 2:
                            sync.wait_ge(se["ev"], i - 1)
                        sync.wait_ge(se["xld"], 16 * i)
                        sync.dma_start(
                            out=x2e_t[i % 2][:, :],
                            in_=x2_sp.ap()[mc * P:(mc + 1) * P,
                                           f * 512:(f + 1) * 512],
                        ).then_inc(se["xld"], 16)
                        i += 1

            @blk.tensor
            def _(te):
                i = 0
                for f in range(NF):
                    for mc in range(NKHH):
                        par = i % 2
                        if i >= 2:
                            te.wait_ge(se["ev"], i - 1)
                        te.matmul(p_f2[par][:, :],
                                  bfl_bf[:, mc * P:(mc + 1) * P],
                                  lows_sb[:, f * 512:(f + 1) * 512],
                                  start=True, stop=True)
                        te.drain()
                        te.sem_inc(se["mmf"], 1)
                        if i >= 1:
                            j = i - 1
                            te.wait_ge(se["sq"], j + 1)
                            te.matmul(p_s1e[:, :], ones_c_sb[:, :],
                                      r2_t[j % 2][:, :],
                                      start=(j % NKHH == 0),
                                      stop=(j % NKHH == NKHH - 1))
                            te.matmul(p_s2e[:, :], ones_c_sb[:, :],
                                      sq_t[j % 2][:, :],
                                      start=(j % NKHH == 0),
                                      stop=(j % NKHH == NKHH - 1))
                            te.drain()
                            te.sem_inc(se["smm"], 1)
                        i += 1
                j = NF * NKHH - 1
                te.wait_ge(se["sq"], j + 1)
                te.matmul(p_s1e[:, :], ones_c_sb[:, :], r2_t[j % 2][:, :],
                          start=(j % NKHH == 0), stop=True)
                te.matmul(p_s2e[:, :], ones_c_sb[:, :], sq_t[j % 2][:, :],
                          start=(j % NKHH == 0), stop=True)
                te.drain()
                te.sem_inc(se["smm"], 1)

            @blk.vector
            def _(v):
                i = 0
                for f in range(NF):
                    for mc in range(NKHH):
                        par = i % 2
                        v.wait_ge(se["mmf"], i + 1)
                        v.wait_ge(se["xld"], 16 * (i + 1))
                        if i >= 2:
                            v.wait_ge(se["r2st"], 16 * (i - 1))
                            v.wait_ge(se["smm"], i - 1)
                        v.tensor_add(r2_t[par][:, :], p_f2[par][:, :],
                                     x2e_t[par][:, :]).then_inc(se["ev"], 1)
                        v.drain()
                        v.tensor_mul(sq_t[par][:, :], r2_t[par][:, :],
                                     r2_t[par][:, :]).then_inc(se["sq"], 1)
                        v.drain()
                        if mc == NKHH - 1:
                            v.wait_ge(se["smm"], i + 1)
                            if f >= 1:
                                v.wait_ge(se["ss"], 32 * f)
                            v.tensor_copy(s1e_t[:, :], p_s1e[:, :])
                            v.tensor_copy(s2e_t[:, :], p_s2e[:, :]).then_inc(se["ssv"], 1)
                            v.drain()
                        i += 1

            @blk.gpsimd
            def _(g):
                i = 0
                for f in range(NF):
                    for mc in range(NKHH):
                        g.wait_ge(se["ev"], i + 1)
                        g.wait_ge(se["r2st"], 16 * i)
                        g.dma_start(
                            out=r2_sp.ap()[mc * P:(mc + 1) * P,
                                           f * 512:(f + 1) * 512],
                            in_=r2_t[i % 2][:, :]).then_inc(se["r2st"], 16)
                        if mc == NKHH - 1:
                            g.wait_ge(se["ssv"], f + 1)
                            g.wait_ge(se["ss"], 32 * f)
                            g.dma_start(out=st2_in.ap()[0:1, f * 512:(f + 1) * 512],
                                        in_=s1e_t[:, :]).then_inc(se["ss"], 16)
                            g.dma_start(out=st2_in.ap()[1:2, f * 512:(f + 1) * 512],
                                        in_=s2e_t[:, :]).then_inc(se["ss"], 16)
                        i += 1
                g.wait_ge(se["r2st"], 16 * NF * NKHH)
                g.wait_ge(se["ss"], 32 * NF)
                g.collective_compute(
                    "AllReduce", ALU.add, replica_groups=PAIRS,
                    ins=[st2_in.ap().opt()], outs=[st2_out.ap().opt()],
                ).then_inc(se["ar"], 1)
                g.wait_ge(se["ar"], 1)
            barrier()
        if stop_after == "E2":
            e_stack.close(); x2_stack.close()
            return nc
        e_stack.close()
        x2_stack.close()

        with contextlib.ExitStack() as ps:
            PS = ps.enter_context
            s1_row = PS(nc.sbuf_tensor("s1f_row", [1, L], F32))
            s2_row = PS(nc.sbuf_tensor("s2f_row", [1, L], F32))
            tmp512 = PS(nc.sbuf_tensor("tmp512f", [1, 512], F32))
            p_bcf = [PS(nc.psum_tensor(f"p_bcf{i}", [P, 512], F32)) for i in range(2)]
            r2f_t = [PS(nc.sbuf_tensor(f"r2f_t{i}", [P, 512], F32)) for i in range(2)]
            o_t = [PS(nc.sbuf_tensor(f"o_t{i}", [P, 512], F32)) for i in range(2)]
            s = sems("Hh", ["ld", "vi", "bcmm", "rld", "o", "ost"])

            @blk.sync
            def _(sync):
                sync.dma_start(out=s1_row[:, :], in_=st2_out.ap()[0:1, :]).then_inc(s["ld"], 16)
                sync.dma_start(out=s2_row[:, :], in_=st2_out.ap()[1:2, :]).then_inc(s["ld"], 16)

            @blk.vector
            def _(v):
                v.wait_ge(s["ld"], 32)
                v.tensor_scalar_mul(s1_row[:, :], s1_row[:, :], 1.0 / H)
                v.drain()
                for f in range(NF):
                    fs = slice(f * 512, (f + 1) * 512)
                    v.tensor_mul(tmp512[:, :], s1_row[:, fs], s1_row[:, fs])
                    v.drain()
                    ins = v.scalar_tensor_tensor(s2_row[:, fs], s2_row[:, fs],
                                                 1.0 / H, tmp512[:, :],
                                                 ALU.mult, ALU.subtract)
                    if f == NF - 1:
                        ins.then_inc(s["vi"], 1)
                    v.drain()

            @blk.scalar
            def _(sc):
                sc.wait_ge(s["vi"], 1)
                sc.activation(s2_row[:, :], s2_row[:, :], AF.Ln, bias=eps_sb[0:1, 1:2])
                sc.drain()
                sc.activation(s2_row[:, :], s2_row[:, :], AF.Exp, scale=-0.5) \
                    .then_inc(s["vi"], 1)
                sc.drain()

            @blk.tensor
            def _(te):
                te.wait_ge(s["vi"], 2)
                for f in range(NF):
                    if f >= 1:
                        te.wait_ge(s["o"], NKHH * f)
                    fs = slice(f * 512, (f + 1) * 512)
                    te.matmul(p_bcf[0][:, :], ones_r_sb[:, :], s1_row[:, fs],
                              start=True, stop=True)
                    te.matmul(p_bcf[1][:, :], ones_r_sb[:, :], s2_row[:, fs],
                              start=True, stop=True)
                    te.drain()
                    te.sem_inc(s["bcmm"], 1)

            @blk.sync
            def _(sync):
                i = 0
                for f in range(NF):
                    for mc in range(NKHH):
                        if i >= 2:
                            sync.wait_ge(s["o"], i - 1)
                        sync.wait_ge(s["rld"], 16 * i)
                        sync.dma_start(
                            out=r2f_t[i % 2][:, :],
                            in_=r2_sp.ap()[mc * P:(mc + 1) * P, f * 512:(f + 1) * 512],
                        ).then_inc(s["rld"], 16)
                        i += 1

            @blk.vector
            def _(v):
                i = 0
                for f in range(NF):
                    v.wait_ge(s["bcmm"], f + 1)
                    for mc in range(NKHH):
                        par = i % 2
                        v.wait_ge(s["rld"], 16 * (i + 1))
                        if i >= 2:
                            v.wait_ge(s["ost"], 16 * (i - 1))
                        v.tensor_sub(o_t[par][:, :], r2f_t[par][:, :],
                                     p_bcf[0][:, :])
                        v.drain()
                        v.tensor_mul(o_t[par][:, :], o_t[par][:, :],
                                     p_bcf[1][:, :]).then_inc(s["o"], 1)
                        v.drain()
                        i += 1

            @blk.gpsimd
            def _(g):
                i = 0
                for f in range(NF):
                    for mc in range(NKHH):
                        g.wait_ge(s["o"], i + 1)
                        g.wait_ge(s["ost"], 16 * i)
                        g.dma_start(
                            out=out_ht.ap()[mc * P:(mc + 1) * P, f * 512:(f + 1) * 512],
                            in_=o_t[i % 2][:, :]).then_inc(s["ost"], 16)
                        i += 1
                g.wait_ge(s["ost"], 16 * NKHH * NF)
            barrier()

    return nc


def _prep_inputs(inputs):
    hs = np.ascontiguousarray(np.asarray(inputs["L0_attn_hidden_states"], dtype=np.float32))
    W_in = np.asarray(inputs["L0_attn_W_in"], dtype=np.float32)
    conv_w = np.asarray(inputs["L0_attn_conv_w"], dtype=np.float32)
    conv_b = np.asarray(inputs["L0_attn_conv_b"], dtype=np.float32)
    Wx = np.asarray(inputs["L0_attn_Wx"], dtype=np.float32)
    dt_ln = np.asarray(inputs["L0_attn_dt_ln_w"], dtype=np.float32)
    b_ln = np.asarray(inputs["L0_attn_b_ln_w"], dtype=np.float32)
    c_ln = np.asarray(inputs["L0_attn_c_ln_w"], dtype=np.float32)
    Wdt = np.asarray(inputs["L0_attn_Wdt"], dtype=np.float32)
    dt_bias = np.asarray(inputs["L0_attn_dt_bias"], dtype=np.float32)
    A = np.asarray(inputs["L0_attn_A"], dtype=np.float32)
    D_param = np.asarray(inputs["L0_attn_D_param"], dtype=np.float32)
    W_out = np.asarray(inputs["L0_attn_W_out"], dtype=np.float32)
    fA = np.asarray(inputs["L0_ffn_A"], dtype=np.float32)
    fB = np.asarray(inputs["L0_ffn_B"], dtype=np.float32)
    fWr = np.asarray(inputs["L0_ffn_Wr"], dtype=np.float32)
    fbr = np.asarray(inputs["L0_ffn_br"], dtype=np.float32)

    def chunk_cols(v, nch):
        return np.ascontiguousarray(v.reshape(nch, P).T)

    ones_r = np.ones((1, P), np.float32)
    ones_c = np.ones((P, 1), np.float32)
    ones16 = np.ones((16, 1), np.float32)
    rrep = np.zeros((8, P), np.float32)
    for h in range(8):
        rrep[h, h * 16:(h + 1) * 16] = 8.0
    indrep = np.zeros((16, 16 * P), ml_dtypes.bfloat16)
    for n in range(16):
        indrep[n, n * P:(n + 1) * P] = 1.0

    A_flatT = fA.reshape(P, H).T
    B_flatT = np.transpose(fB, (0, 2, 1)).reshape(P, H)
    WrT = fWr.T
    W_outT = W_out.T
    WxT = Wx.T
    WdtT = Wdt.T

    in_maps = []
    for c in range(8):
        b = c // 2
        hf = c % 2
        ds = slice(hf * DH, (hf + 1) * DH)
        hsl = slice(hf * HH, (hf + 1) * HH)
        xTc = np.ascontiguousarray(hs[b].T)
        Wcat = np.concatenate([W_in[ds], W_in[D + hf * DH: D + (hf + 1) * DH]], 0)
        w_in_tiles = np.ascontiguousarray(
            Wcat.T.reshape(NKH, P, 2 * NDC, P).transpose(2, 1, 0, 3)
        ).reshape(2 * NDC, P, NKH * P)
        wx_c = np.ascontiguousarray(
            WxT[ds].reshape(NKH, P, 288).transpose(1, 0, 2)).reshape(P, NKH * 288)
        wdt_c = np.ascontiguousarray(
            WdtT[:, ds].reshape(2, P, DH).transpose(1, 0, 2)).reshape(P, 2 * DH)
        w_out_tiles = np.ascontiguousarray(
            W_outT[:, hsl].reshape(NKD, P, NKHH, P).transpose(2, 1, 0, 3)
        ).reshape(NKHH, P, NKD * P)
        a_flat_c = np.ascontiguousarray(
            A_flatT[hsl].reshape(NKHH, P, P).transpose(1, 0, 2)).reshape(P, NKHH * P)
        wr_c = np.ascontiguousarray(
            WrT[hsl].reshape(NKHH, P, 8).transpose(1, 0, 2)).reshape(P, NKHH * 8)

        m = {
            "xT": xTc,
            "x_res": np.ascontiguousarray(xTc[hsl]),
            "w_in_t": w_in_tiles,
            "conv_w_t": np.ascontiguousarray(
                conv_w[ds].reshape(NDC, P, KC).transpose(1, 0, 2)).reshape(P, NDC * KC),
            "conv_b_t": chunk_cols(conv_b[ds], NDC),
            "wx_t": wx_c,
            "wdt_t": wdt_c,
            "dt_ln_t": chunk_cols(dt_ln, 2),
            "b_ln_t": np.ascontiguousarray(b_ln.reshape(16, 1)),
            "c_ln_t": np.ascontiguousarray(c_ln.reshape(16, 1)),
            "dt_bias_t": chunk_cols(dt_bias[ds], NDC),
            "a_t": np.ascontiguousarray(
                A[ds].reshape(NDC, P, N).transpose(1, 0, 2)).reshape(P, NDC * N),
            "d_param_t": chunk_cols(D_param[ds], NDC),
            "w_out_t": w_out_tiles,
            "a_flat_t": a_flat_c,
            "b_flat_t": np.ascontiguousarray(B_flatT[:, hsl]),
            "wr_t": wr_c,
            "br_t": np.ascontiguousarray(fbr.reshape(8, 1)),
            "rrep_t": rrep,
            "indrep_t": indrep,
            "ones_r": ones_r,
            "ones_c": ones_c,
            "ones16": ones16,
            "eps_t": np.array([[EPS_RMS, EPS_LN]], np.float32),
        }
        in_maps.append(m)
    return in_maps


def kernel(**inputs):
    in_maps = _prep_inputs(inputs)
    nc = build_nc(debug=False)
    res = run_bass_kernel_spmd(nc, in_maps, core_ids=list(range(8)))
    out = np.empty((B_, L, H), np.float32)
    for c in range(8):
        b, hf = c // 2, c % 2
        out[b, :, hf * HH:(hf + 1) * HH] = np.asarray(res.results[c]["out_ht"]).T
    return out


# version 05 (final)
# speedup vs baseline: 1.1627x; 1.1627x over previous
import contextlib
import numpy as np
import ml_dtypes
import concourse.bass as bass
import concourse.mybir as mybir
from concourse.bass_utils import run_bass_kernel_spmd

F32 = mybir.dt.float32
BF16 = mybir.dt.bfloat16
AF = mybir.ActivationFunctionType
ALU = mybir.AluOpType
AX = mybir.AxisListType

P = 128
B_, L, H, D, N, KC, R = 4, 4096, 4096, 8192, 16, 4, 256
DH = D // 2
HH = H // 2
NDC = DH // P
NKH = H // P
NKHH = HH // P
NKD = D // P
TC = 512
NTC = L // TC
TH = L // 2
NF = L // 512
EPS_RMS = 1e-6
EPS_LN = 1e-5
ENGINES = ("sync", "scalar", "vector", "gpsimd", "tensor")
C_NTC = NTC
C_SKIP = set()
PAIRS = [[0, 1], [2, 3], [4, 5], [6, 7]]


def build_nc(debug=False, stop_after=None):
    nc = bass.Bass("TRN2", target_bir_lowering=False, debug=False)
    dram = nc.dram_tensor
    spk = "ExternalOutput" if debug else "Internal"

    xT = dram("xT", [H, L], F32, kind="ExternalInput")
    x_res = dram("x_res", [HH, L], F32, kind="ExternalInput")
    w_in_t = dram("w_in_t", [2 * NDC, P, NKH * P], F32, kind="ExternalInput")
    conv_w_t = dram("conv_w_t", [P, NDC * KC], F32, kind="ExternalInput")
    conv_b_t = dram("conv_b_t", [P, NDC], F32, kind="ExternalInput")
    wx_t = dram("wx_t", [P, NKH * 288], F32, kind="ExternalInput")
    wdt_t = dram("wdt_t", [P, 2 * DH], F32, kind="ExternalInput")
    dt_ln_t = dram("dt_ln_t", [P, 2], F32, kind="ExternalInput")
    b_ln_t = dram("b_ln_t", [16, 1], F32, kind="ExternalInput")
    c_ln_t = dram("c_ln_t", [16, 1], F32, kind="ExternalInput")
    dt_bias_t = dram("dt_bias_t", [P, NDC], F32, kind="ExternalInput")
    a_t = dram("a_t", [P, NDC * N], F32, kind="ExternalInput")
    d_param_t = dram("d_param_t", [P, NDC], F32, kind="ExternalInput")
    w_out_t = dram("w_out_t", [NKHH, P, NKD * P], F32, kind="ExternalInput")
    a_flat_t = dram("a_flat_t", [P, NKHH * P], F32, kind="ExternalInput")
    b_flat_t = dram("b_flat_t", [P, HH], F32, kind="ExternalInput")
    wr_t = dram("wr_t", [P, NKHH * 8], F32, kind="ExternalInput")
    br_t = dram("br_t", [8, 1], F32, kind="ExternalInput")
    rrep_t = dram("rrep_t", [8, P], F32, kind="ExternalInput")
    indrep_t = dram("indrep_t", [16, 16 * P], BF16, kind="ExternalInput")
    ones_r_t = dram("ones_r", [1, P], F32, kind="ExternalInput")
    ones_c_t = dram("ones_c", [P, 1], F32, kind="ExternalInput")
    ones16_t = dram("ones16", [16, 1], F32, kind="ExternalInput")
    eps_t = dram("eps_t", [1, 2], F32, kind="ExternalInput")

    out_ht = dram("out_ht", [HH, L], F32, kind="ExternalOutput")

    ub_sp = dram("ub_sp", [DH, L], BF16, kind=spk)
    gate_sp = dram("gate_sp", [DH, L], BF16, kind=spk)
    delta_sp = dram("delta_sp", [DH, L], F32, kind=spk)
    ssm_in = dram("ssm_in", [288, L], F32, kind="Internal")
    ssm_out = dram("ssm_out", [288, L], F32, kind="Internal")
    ssm_dbg = dram("ssm_dbg", [288, L], F32, kind=spk) if debug else None
    mskb_dbg = dram("mskb_dbg", [P, L], F32, kind=spk) if debug else None
    brow_dbg = dram("brow_dbg", [16, L], F32, kind=spk) if debug else None
    crow_dbg = dram("crow_dbg", [16, L], F32, kind=spk) if debug else None
    yag_in = [dram(f"yag_in{i}", [DH, 2 * TC], BF16, kind="Internal")
              for i in range(NTC // 2)]
    yag_dbg = [dram(f"yag_dbg{i}", [DH, 2 * TC], BF16, kind=spk)
               for i in range(NTC // 2)] if debug else None
    yag_out = [dram(f"yag_out{i}", [D, 2 * TC], BF16, kind="Internal")
               for i in range(NTC // 2)]
    wob_sp = dram("wob_sp", [NKHH, P, NKD * P], BF16, kind="Internal")
    r1_sp = dram("r1_sp", [HH, L], F32, kind=spk)
    x2_sp = dram("x2_sp", [HH, L], F32, kind=spk)
    r2_sp = dram("r2_sp", [HH, L], F32, kind="Internal")
    st1_in = dram("st1_in", [2, L], F32, kind="Internal")
    st1_out = dram("st1_out", [2, L], F32, kind="Internal")
    rtr_in = dram("rtr_in", [8, L], F32, kind="Internal")
    rtr_out = dram("rtr_out", [8, L], F32, kind="Internal")
    low_in = dram("low_in", [P, L], F32, kind="Internal")
    low_out = dram("low_out", [P, L], F32, kind="Internal")
    low_dbg = dram("low_dbg", [P, L], F32, kind=spk) if debug else None
    st2_in = dram("st2_in", [2, L], F32, kind="Internal")
    st2_out = dram("st2_out", [2, L], F32, kind="Internal")

    es = contextlib.ExitStack()
    with es:
        E = es.enter_context
        blk = E(nc.Block())
        phz = E(nc.semaphore("phz"))
        dbgg = E(nc.semaphore("dbgg")) if debug else None
        phase_idx = [0]

        def sems(prefix, names):
            return {n: E(nc.semaphore(prefix + n)) for n in names}

        def barrier():
            phase_idx[0] += 1
            target = len(ENGINES) * phase_idx[0]
            for name in ENGINES:
                @getattr(blk, name)
                def _(eng):
                    eng.drain()
                    eng.sem_inc(phz, 1)
                    eng.wait_ge(phz, target)

        conv_w_sb = E(nc.sbuf_tensor("conv_w_sb", [P, NDC * KC], F32))
        conv_b_sb = E(nc.sbuf_tensor("conv_b_sb", [P, NDC], F32))
        dt_bias_sb = E(nc.sbuf_tensor("dt_bias_sb", [P, NDC], F32))
        a_sb = E(nc.sbuf_tensor("a_sb", [P, NDC * N], F32))
        d_param_sb = E(nc.sbuf_tensor("d_param_sb", [P, NDC], F32))
        halo_sb = E(nc.sbuf_tensor("halo_sb", [P, NDC * 3], F32))
        state_sb = E(nc.sbuf_tensor("state_sb", [P, NDC * N], F32))
        indrep_sb = E(nc.sbuf_tensor("indrep_sb", [16, 16 * P], BF16))
        ones_r_sb = E(nc.sbuf_tensor("ones_r_sb", [1, P], F32))
        ones_c_sb = E(nc.sbuf_tensor("ones_c_sb", [P, 1], F32))
        ones16_sb = E(nc.sbuf_tensor("ones16_sb", [16, 1], F32))
        b_ln_sb = E(nc.sbuf_tensor("b_ln_sb", [16, 1], F32))
        c_ln_sb = E(nc.sbuf_tensor("c_ln_sb", [16, 1], F32))
        br_sb = E(nc.sbuf_tensor("br_sb", [8, 1], F32))
        rrep_sb = E(nc.sbuf_tensor("rrep_sb", [8, P], F32))
        dt_ln_sb = E(nc.sbuf_tensor("dt_ln_sb", [P, 2], F32))
        eps_sb = E(nc.sbuf_tensor("eps_sb", [1, 2], F32))

        p0d = E(nc.semaphore("p0d"))
        loads0 = [
            (conv_w_sb, conv_w_t), (conv_b_sb, conv_b_t),
            (dt_bias_sb, dt_bias_t), (a_sb, a_t),
            (d_param_sb, d_param_t), (indrep_sb, indrep_t),
            (ones_r_sb, ones_r_t), (ones_c_sb, ones_c_t),
            (ones16_sb, ones16_t), (b_ln_sb, b_ln_t),
            (c_ln_sb, c_ln_t), (br_sb, br_t), (rrep_sb, rrep_t),
            (dt_ln_sb, dt_ln_t), (eps_sb, eps_t),
        ]

        @blk.sync
        def _(sync):
            for sb, dr in loads0:
                sync.dma_start(out=sb[:, :], in_=dr.ap()).then_inc(p0d, 16)
            sync.wait_ge(p0d, 16 * len(loads0))

        @blk.vector
        def _(v):
            v.memset(halo_sb[:, :], 0.0)
            v.memset(state_sb[:, :], 0.0)
        barrier()

        if stop_after == "P0":
            return nc
        for th in range(2):
            t0 = th * TH
            with contextlib.ExitStack() as ps:
                PS = ps.enter_context
                nm = f"A{th}"
                x_stage = [PS(nc.sbuf_tensor(f"{nm}x_stage{i}", [P, TH // 2], F32)) for i in range(2)]
                x_bf = PS(nc.sbuf_tensor(f"{nm}x_bf", [P, NKH * TH], BF16))
                wstage = [PS(nc.sbuf_tensor(f"{nm}wstage{i}", [P, NKH * P // 2], F32)) for i in range(2)]
                wbf = [PS(nc.sbuf_tensor(f"{nm}wbf{i}", [P, NKH * P], BF16)) for i in range(2)]
                pu = PS(nc.psum_tensor(f"{nm}pu", [P, TH], F32))
                pg = PS(nc.psum_tensor(f"{nm}pg", [P, TH], F32))
                cacc = PS(nc.sbuf_tensor(f"{nm}cacc", [P, TH], F32))
                u_b = [PS(nc.sbuf_tensor(f"{nm}u_b{i}", [P, TH], BF16)) for i in range(2)]
                g_b = [PS(nc.sbuf_tensor(f"{nm}g_b{i}", [P, TH], BF16)) for i in range(2)]
                s = sems(nm, ["ld", "c", "mmu", "mmg",
                              "cv", "uf", "gf", "st"])
                HW = NKH * P // 2

                @blk.sync
                def _(sync):
                    for k in range(NKH):
                        for hv in range(2):
                            j = 2 * k + hv
                            if j >= 2:
                                sync.wait_ge(s["c"], j - 1)
                            sync.wait_ge(s["ld"], 16 * j)
                            sync.dma_start(
                                out=x_stage[j % 2][:, :],
                                in_=xT.ap()[k * P:(k + 1) * P,
                                            t0 + hv * (TH // 2):
                                            t0 + (hv + 1) * (TH // 2)],
                            ).then_inc(s["ld"], 16)
                    for i in range(2 * NDC):
                        dc = i // 2
                        mc = dc if i % 2 == 0 else NDC + dc
                        for hv in range(2):
                            j = 2 * i + hv
                            if j >= 2:
                                sync.wait_ge(s["c"], 2 * NKH + j - 1)
                            sync.wait_ge(s["ld"], 16 * (2 * NKH + j))
                            sync.dma_start(
                                out=wstage[j % 2][:, :],
                                in_=w_in_t.ap()[mc][:, hv * HW:(hv + 1) * HW],
                            ).then_inc(s["ld"], 16)

                @blk.vector
                def _(v):
                    for k in range(NKH):
                        for hv in range(2):
                            j = 2 * k + hv
                            v.wait_ge(s["ld"], 16 * (j + 1))
                            v.tensor_copy(
                                x_bf[:, k * TH + hv * (TH // 2):
                                     k * TH + (hv + 1) * (TH // 2)],
                                x_stage[j % 2][:, :]).then_inc(s["c"], 1)
                    v.drain()

                    def wcast(i):
                        for hv in range(2):
                            j = 2 * i + hv
                            v.wait_ge(s["ld"], 16 * (2 * NKH + j + 1))
                            if i >= 2 and hv == 0:
                                t = i - 2
                                if t % 2 == 0:
                                    v.wait_ge(s["mmu"], t // 2 + 1)
                                else:
                                    v.wait_ge(s["mmg"], t // 2 + 1)
                            v.tensor_copy(wbf[i % 2][:, hv * HW:(hv + 1) * HW],
                                          wstage[j % 2][:, :]).then_inc(s["c"], 1)
                            v.drain()

                    wcast(0)
                    wcast(1)
                    cw = conv_w_sb
                    for dc in range(NDC):
                        v.wait_ge(s["mmu"], dc + 1)
                        if dc >= 1:
                            v.wait_ge(s["uf"], dc)
                        w0 = cw[:, dc * 4 + 0:dc * 4 + 1]
                        w1 = cw[:, dc * 4 + 1:dc * 4 + 2]
                        w2 = cw[:, dc * 4 + 2:dc * 4 + 3]
                        w3 = cw[:, dc * 4 + 3:dc * 4 + 4]
                        hal = halo_sb[:, dc * 3:(dc + 1) * 3]
                        v.tensor_scalar_mul(cacc[:, :], pu[:, :], w3)
                        v.drain()
                        v.scalar_tensor_tensor(cacc[:, 1:TH], pu[:, 0:TH - 1], w2,
                                               cacc[:, 1:TH], ALU.mult, ALU.add)
                        v.drain()
                        v.scalar_tensor_tensor(cacc[:, 0:1], hal[:, 2:3], w2,
                                               cacc[:, 0:1], ALU.mult, ALU.add)
                        v.drain()
                        v.scalar_tensor_tensor(cacc[:, 2:TH], pu[:, 0:TH - 2], w1,
                                               cacc[:, 2:TH], ALU.mult, ALU.add)
                        v.drain()
                        v.scalar_tensor_tensor(cacc[:, 0:2], hal[:, 1:3], w1,
                                               cacc[:, 0:2], ALU.mult, ALU.add)
                        v.drain()
                        v.scalar_tensor_tensor(cacc[:, 3:TH], pu[:, 0:TH - 3], w0,
                                               cacc[:, 3:TH], ALU.mult, ALU.add)
                        v.drain()
                        v.scalar_tensor_tensor(cacc[:, 0:3], hal[:, 0:3], w0,
                                               cacc[:, 0:3], ALU.mult, ALU.add)
                        v.drain()
                        v.tensor_copy(hal, pu[:, TH - 3:TH]).then_inc(s["cv"], 1)
                        v.drain()
                        if 2 * dc + 2 < 2 * NDC:
                            wcast(2 * dc + 2)
                        if 2 * dc + 3 < 2 * NDC:
                            wcast(2 * dc + 3)

                @blk.tensor
                def _(te):
                    for i in range(2 * NDC):
                        dc = i // 2
                        kind = "u" if i % 2 == 0 else "g"
                        te.wait_ge(s["c"], 2 * NKH + 2 * (i + 1))
                        if kind == "u":
                            if dc >= 1:
                                te.wait_ge(s["cv"], dc)
                            pt = pu
                        else:
                            if dc >= 1:
                                te.wait_ge(s["gf"], dc)
                            pt = pg
                        for k in range(NKH):
                            for f in range(TH // 512):
                                ins = te.matmul(
                                    pt[:, f * 512:(f + 1) * 512],
                                    wbf[i % 2][:, k * P:(k + 1) * P],
                                    x_bf[:, k * TH + f * 512:k * TH + (f + 1) * 512],
                                    start=(k == 0), stop=(k == NKH - 1),
                                )
                                pass
                        te.drain()
                        te.sem_inc(s["mmu" if kind == "u" else "mmg"], 1)

                @blk.scalar
                def _(sc):
                    for dc in range(NDC):
                        buf = dc % 2
                        sc.wait_ge(s["cv"], dc + 1)
                        if dc >= 2:
                            sc.wait_ge(s["st"], 16 * 2 * (dc - 1))
                        sc.activation(u_b[buf][:, :], cacc[:, :], AF.Silu,
                                      bias=conv_b_sb[:, dc:dc + 1]).then_inc(s["uf"], 1)
                        sc.wait_ge(s["mmg"], dc + 1)
                        sc.activation(g_b[buf][:, :], pg[:, :], AF.Silu) \
                            .then_inc(s["gf"], 1)
                        sc.drain()

                @blk.gpsimd
                def _(g):
                    for dc in range(NDC):
                        buf = dc % 2
                        g.wait_ge(s["uf"], dc + 1)
                        g.wait_ge(s["st"], 16 * 2 * dc)
                        g.dma_start(out=ub_sp.ap()[dc * P:(dc + 1) * P, t0:t0 + TH],
                                    in_=u_b[buf][:, :]).then_inc(s["st"], 16)
                        g.wait_ge(s["gf"], dc + 1)
                        g.wait_ge(s["st"], 16 * (2 * dc + 1))
                        g.dma_start(out=gate_sp.ap()[dc * P:(dc + 1) * P, t0:t0 + TH],
                                    in_=g_b[buf][:, :]).then_inc(s["st"], 16)
                    g.wait_ge(s["st"], 16 * 2 * NDC)
                barrier()

        if stop_after == "A":
            return nc
        bc_stack = contextlib.ExitStack()
        BCE = bc_stack.enter_context
        brow_sb = BCE(nc.sbuf_tensor("brow_sb", [16, L], BF16))
        crow_sb = BCE(nc.sbuf_tensor("crow_sb", [16, L], BF16))
        b_stack = contextlib.ExitStack()
        BE = b_stack.enter_context
        dt0b_sb = BE(nc.sbuf_tensor("dt0b_sb", [P, L], BF16))
        dt1b_sb = BE(nc.sbuf_tensor("dt1b_sb", [P, L], BF16))
        mskb_bc = BE(nc.sbuf_tensor("mskb_bc", [P, L], F32))

        with contextlib.ExitStack() as ps:
            PS = ps.enter_context
            wx_stage = PS(nc.sbuf_tensor("wx_stage", [P, NKH * 288 // 2], F32))
            wx_bf = PS(nc.sbuf_tensor("wx_bf", [P, NKH * 288], BF16))
            ub_tile = [PS(nc.sbuf_tensor(f"ub_tile{i}", [P, 512], BF16)) for i in range(3)]
            p_dt0 = PS(nc.psum_tensor("p_dt0", [P, 512], F32))
            p_dt1 = PS(nc.psum_tensor("p_dt1", [P, 512], F32))
            p_bcB = PS(nc.psum_tensor("p_bcB", [16, 512], F32))
            p_bcC = PS(nc.psum_tensor("p_bcC", [16, 512], F32))
            t_dt0 = [PS(nc.sbuf_tensor(f"t_dt0{i}", [P, 512], F32)) for i in range(2)]
            t_dt1 = [PS(nc.sbuf_tensor(f"t_dt1{i}", [P, 512], F32)) for i in range(2)]
            t_bcB = [PS(nc.sbuf_tensor(f"t_bcB{i}", [16, 512], F32)) for i in range(2)]
            t_bcC = [PS(nc.sbuf_tensor(f"t_bcC{i}", [16, 512], F32)) for i in range(2)]
            s = sems("B", ["wld", "wc", "uld0", "uld1", "uld2", "uf",
                           "ev2", "shp", "ar"])
            WXH = NKH * 288 // 2

            @blk.sync
            def _(sync):
                for hv in range(2):
                    if hv == 1:
                        sync.wait_ge(s["wc"], 1)
                    sync.dma_start(out=wx_stage[:, :],
                                   in_=wx_t.ap()[:, hv * WXH:(hv + 1) * WXH]) \
                        .then_inc(s["wld"], 16)
                i = 0
                for f in range(NF):
                    for k in range(NKH):
                        if i >= 3:
                            sync.wait_ge(s["uf"], i - 2)
                        sync.dma_start(
                            out=ub_tile[i % 3][:, :],
                            in_=ub_sp.ap()[k * P:(k + 1) * P, f * 512:(f + 1) * 512],
                        ).then_inc(s["uld%d" % (i % 3)], 16)
                        i += 1

            @blk.scalar
            def _(sc):
                for hv in range(2):
                    sc.wait_ge(s["wld"], 16 * (hv + 1))
                    sc.copy(wx_bf[:, hv * WXH:(hv + 1) * WXH], wx_stage[:, :]) \
                        .then_inc(s["wc"], 1)
                    sc.drain()

            @blk.tensor
            def _(te):
                te.wait_ge(s["wc"], 2)
                i = 0
                for f in range(NF):
                    if f >= 1:
                        te.wait_ge(s["ev2"], f)
                    for k in range(NKH):
                        te.wait_ge(s["uld%d" % (i % 3)], 16 * (i // 3 + 1))
                        rhs = ub_tile[i % 3][:, :]
                        st, sp = (k == 0), (k == NKH - 1)
                        te.matmul(p_dt0[:, :], wx_bf[:, k * 288:k * 288 + P], rhs,
                                  start=st, stop=sp)
                        te.matmul(p_dt1[:, :], wx_bf[:, k * 288 + P:k * 288 + 2 * P],
                                  rhs, start=st, stop=sp)
                        te.matmul(p_bcB[:, :], wx_bf[:, k * 288 + 256:k * 288 + 272],
                                  rhs, start=st, stop=sp)
                        te.matmul(p_bcC[:, :],
                                  wx_bf[:, k * 288 + 272:k * 288 + 288],
                                  rhs, start=st, stop=sp)
                        te.drain()
                        te.sem_inc(s["uf"], 1)
                        i += 1

            @blk.scalar
            def _(sc):
                for f in range(NF):
                    par = f % 2
                    sc.wait_ge(s["uf"], NKH * (f + 1))
                    if f >= 2:
                        sc.wait_ge(s["shp"], 16 * 4 * (f - 1))
                    sc.copy(t_dt0[par][:, :], p_dt0[:, :])
                    sc.copy(t_dt1[par][:, :], p_dt1[:, :])
                    sc.copy(t_bcB[par][:, :], p_bcB[:, :])
                    sc.copy(t_bcC[par][:, :], p_bcC[:, :]).then_inc(s["ev2"], 1)
                    sc.drain()

            @blk.gpsimd
            def _(g):
                for f in range(NF):
                    par = f % 2
                    fs = slice(f * 512, (f + 1) * 512)
                    g.wait_ge(s["ev2"], f + 1)
                    g.wait_ge(s["shp"], 64 * f)
                    g.dma_start(out=ssm_in.ap()[0:P, fs], in_=t_dt0[par][:, :]) \
                        .then_inc(s["shp"], 16)
                    g.dma_start(out=ssm_in.ap()[P:2 * P, fs], in_=t_dt1[par][:, :]) \
                        .then_inc(s["shp"], 16)
                    g.dma_start(out=ssm_in.ap()[256:272, fs], in_=t_bcB[par][:, :]) \
                        .then_inc(s["shp"], 16)
                    g.dma_start(out=ssm_in.ap()[272:288, fs], in_=t_bcC[par][:, :]) \
                        .then_inc(s["shp"], 16)
                g.wait_ge(s["shp"], 16 * 4 * NF)
                g.collective_compute(
                    "AllReduce", ALU.add, replica_groups=PAIRS,
                    ins=[ssm_in.ap().opt()], outs=[ssm_out.ap().opt()],
                ).then_inc(s["ar"], 1)
                g.wait_ge(s["ar"], 1)
                if debug:
                    g.dma_start(out=ssm_dbg.ap(), in_=ssm_out.ap()) \
                        .then_inc(dbgg, 16)
                    g.wait_ge(dbgg, 16)
            barrier()

        if stop_after == "B1":
            b_stack.close(); bc_stack.close()
            return nc
        with contextlib.ExitStack() as ps:
            PS = ps.enter_context
            rdt0 = [PS(nc.sbuf_tensor(f"rdt0{i}", [P, 512], F32)) for i in range(2)]
            rdt1 = [PS(nc.sbuf_tensor(f"rdt1{i}", [P, 512], F32)) for i in range(2)]
            rbcB = [PS(nc.sbuf_tensor(f"rbcB{i}", [16, 512], F32)) for i in range(2)]
            rbcC = [PS(nc.sbuf_tensor(f"rbcC{i}", [16, 512], F32)) for i in range(2)]
            qdt = PS(nc.sbuf_tensor("qdt", [P, 512], F32))
            qdt2 = PS(nc.sbuf_tensor("qdt2", [P, 512], F32))
            qbc = PS(nc.sbuf_tensor("qbc", [16, 512], F32))
            qbc2 = PS(nc.sbuf_tensor("qbc2", [16, 512], F32))
            p_st = PS(nc.psum_tensor("p_st", [1, 512], F32))
            p_stB = PS(nc.psum_tensor("p_stB", [1, 512], F32))
            p_stC = PS(nc.psum_tensor("p_stC", [1, 512], F32))
            ss_row = PS(nc.sbuf_tensor("ss_row", [1, L], F32))
            ssB_row = PS(nc.sbuf_tensor("ssB_row", [1, L], F32))
            ssC_row = PS(nc.sbuf_tensor("ssC_row", [1, L], F32))
            p_bc = PS(nc.psum_tensor("p_bc", [P, 512], F32))
            p_bc16 = PS(nc.psum_tensor("p_bc16", [16, 512], F32))
            s = sems("Bb", ["rld", "sq", "smm", "sev", "msk", "bcmm", "bcev",
                            "rld2"])

            @blk.sync
            def _(sync):
                for f in range(NF):
                    par = f % 2
                    fs = slice(f * 512, (f + 1) * 512)
                    if f >= 2:
                        sync.wait_ge(s["sq"], f - 1)
                    sync.wait_ge(s["rld"], 64 * f)
                    sync.dma_start(out=rdt0[par][:, :], in_=ssm_out.ap()[0:P, fs]) \
                        .then_inc(s["rld"], 16)
                    sync.dma_start(out=rdt1[par][:, :], in_=ssm_out.ap()[P:2 * P, fs]) \
                        .then_inc(s["rld"], 16)
                    sync.dma_start(out=rbcB[par][:, :], in_=ssm_out.ap()[256:272, fs]) \
                        .then_inc(s["rld"], 16)
                    sync.dma_start(out=rbcC[par][:, :], in_=ssm_out.ap()[272:288, fs]) \
                        .then_inc(s["rld"], 16)
                sync.wait_ge(s["sq"], NF)
                for f in range(NF):
                    par = f % 2
                    fs = slice(f * 512, (f + 1) * 512)
                    if f >= 2:
                        sync.wait_ge(s["bcev"], 3 * (f - 1))
                    sync.wait_ge(s["rld2"], 32 * f)
                    sync.dma_start(out=rbcB[par][:, :], in_=ssm_out.ap()[256:272, fs]) \
                        .then_inc(s["rld2"], 16)
                    sync.dma_start(out=rbcC[par][:, :], in_=ssm_out.ap()[272:288, fs]) \
                        .then_inc(s["rld2"], 16)

            @blk.scalar
            def _(sc):
                for f in range(NF):
                    par = f % 2
                    fs = slice(f * 512, (f + 1) * 512)
                    sc.wait_ge(s["rld"], 16 * 4 * (f + 1))
                    if f >= 1:
                        sc.wait_ge(s["smm"], f)
                    sc.square(qdt[:, :], rdt0[par][:, :])
                    sc.square(qdt2[:, :], rdt1[par][:, :])
                    sc.square(qbc[:, :], rbcB[par][:, :])
                    sc.square(qbc2[:, :], rbcC[par][:, :])
                    sc.copy(dt0b_sb[:, fs], rdt0[par][:, :])
                    sc.copy(dt1b_sb[:, fs], rdt1[par][:, :]).then_inc(s["sq"], 1)
                    sc.drain()

            @blk.tensor
            def _(te):
                for f in range(NF):
                    te.wait_ge(s["sq"], f + 1)
                    if f >= 1:
                        te.wait_ge(s["sev"], f)
                    te.matmul(p_st[:, :], ones_c_sb[:, :], qdt[:, :],
                              start=True, stop=False)
                    te.matmul(p_st[:, :], ones_c_sb[:, :], qdt2[:, :],
                              start=False, stop=True)
                    te.matmul(p_stB[:, :], ones16_sb[:, :], qbc[:, :],
                              start=True, stop=True)
                    te.matmul(p_stC[:, :], ones16_sb[:, :], qbc2[:, :],
                              start=True, stop=True)
                    te.drain()
                    te.sem_inc(s["smm"], 1)

            @blk.vector
            def _(v):
                for f in range(NF):
                    fs = slice(f * 512, (f + 1) * 512)
                    v.wait_ge(s["smm"], f + 1)
                    v.tensor_copy(ss_row[:, fs], p_st[:, :])
                    v.tensor_copy(ssB_row[:, fs], p_stB[:, :])
                    v.tensor_copy(ssC_row[:, fs], p_stC[:, :]).then_inc(s["sev"], 1)
                v.drain()

            @blk.scalar
            def _(sc):
                sc.wait_ge(s["sev"], NF)
                sc.activation(ss_row[:, :], ss_row[:, :], AF.Ln,
                              scale=1.0 / R, bias=eps_sb[0:1, 0:1])
                sc.drain()
                sc.activation(ss_row[:, :], ss_row[:, :], AF.Exp, scale=-0.5)
                sc.drain()
                sc.activation(ssB_row[:, :], ssB_row[:, :], AF.Ln,
                              scale=1.0 / N, bias=eps_sb[0:1, 0:1])
                sc.drain()
                sc.activation(ssB_row[:, :], ssB_row[:, :], AF.Exp, scale=-0.5)
                sc.drain()
                sc.activation(ssC_row[:, :], ssC_row[:, :], AF.Ln,
                              scale=1.0 / N, bias=eps_sb[0:1, 0:1])
                sc.drain()
                sc.activation(ssC_row[:, :], ssC_row[:, :], AF.Exp, scale=-0.5) \
                    .then_inc(s["msk"], 1)
                sc.drain()

            @blk.tensor
            def _(te):
                te.wait_ge(s["msk"], 1)
                for f in range(NF):
                    if f >= 1:
                        te.wait_ge(s["bcev"], 3 * f)
                    fs = slice(f * 512, (f + 1) * 512)
                    te.matmul(p_bc[:, :], ones_r_sb[:, :], ss_row[:, fs],
                              start=True, stop=True)
                    te.matmul(p_bc16[:, :], ones_r_sb[:, 0:16], ssB_row[:, fs],
                              start=True, stop=True)
                    te.drain()
                    te.sem_inc(s["bcmm"], 1)
                    te.wait_ge(s["bcev"], 3 * f + 2)
                    te.matmul(p_bc16[:, :], ones_r_sb[:, 0:16], ssC_row[:, fs],
                              start=True, stop=True)
                    te.drain()
                    te.sem_inc(s["bcmm"], 1)

            @blk.vector
            def _(v):
                for f in range(NF):
                    par = f % 2
                    fs = slice(f * 512, (f + 1) * 512)
                    v.wait_ge(s["bcmm"], 2 * f + 1)
                    v.tensor_copy(mskb_bc[:, fs], p_bc[:, :]).then_inc(s["bcev"], 1)
                    v.drain()
                    v.wait_ge(s["rld2"], 16 * 2 * (f + 1))
                    v.scalar_tensor_tensor(brow_sb[:, fs], rbcB[par][:, :],
                                           b_ln_sb[:, :], p_bc16[:, :],
                                           ALU.mult, ALU.mult).then_inc(s["bcev"], 1)
                    v.drain()
                    v.wait_ge(s["bcmm"], 2 * f + 2)
                    v.scalar_tensor_tensor(crow_sb[:, fs], rbcC[par][:, :],
                                           c_ln_sb[:, :], p_bc16[:, :],
                                           ALU.mult, ALU.mult).then_inc(s["bcev"], 1)
                    v.drain()

            if debug:
                @blk.gpsimd
                def _(g):
                    g.wait_ge(s["bcev"], 3 * NF)
                    g.dma_start(out=mskb_dbg.ap(), in_=mskb_bc[:, :]).then_inc(dbgg, 16)
                    g.wait_ge(dbgg, 32)
                    g.dma_start(out=brow_dbg.ap(), in_=brow_sb[:, :]).then_inc(dbgg, 16)
                    g.wait_ge(dbgg, 48)
                    g.dma_start(out=crow_dbg.ap(), in_=crow_sb[:, :]).then_inc(dbgg, 16)
                    g.wait_ge(dbgg, 64)
            barrier()

        if stop_after == "B2":
            b_stack.close(); bc_stack.close()
            return nc
        with contextlib.ExitStack() as ps:
            PS = ps.enter_context
            wdt_stage = PS(nc.sbuf_tensor("wdt_stage", [P, DH], F32))
            wdt_bf = PS(nc.sbuf_tensor("wdt_bf", [P, 2 * DH], BF16))
            p_m3 = [PS(nc.psum_tensor(f"p_m3{i}", [P, 512], F32)) for i in range(2)]
            dpre = [PS(nc.sbuf_tensor(f"dpre{i}", [P, 512], F32)) for i in range(2)]
            dexp = [PS(nc.sbuf_tensor(f"dexp{i}", [P, 512], F32)) for i in range(2)]
            dout = [PS(nc.sbuf_tensor(f"dout{i}", [P, 512], F32)) for i in range(2)]
            s = sems("Bc", ["wld", "wc", "mm3", "pre3", "d3", "dst"])

            @blk.sync
            def _(sync):
                for kk in range(2):
                    if kk == 1:
                        sync.wait_ge(s["wc"], 1)
                    sync.dma_start(out=wdt_stage[:, :],
                                   in_=wdt_t.ap()[:, kk * DH:(kk + 1) * DH]) \
                        .then_inc(s["wld"], 16)

            @blk.scalar
            def _(sc):
                for kk in range(2):
                    sc.wait_ge(s["wld"], 16 * (kk + 1))
                    sc.activation(wdt_bf[:, kk * DH:(kk + 1) * DH], wdt_stage[:, :],
                                  AF.Copy, scale=dt_ln_sb[:, kk:kk + 1]) \
                        .then_inc(s["wc"], 1)
                    sc.drain()

            @blk.tensor
            def _(te):
                te.wait_ge(s["wc"], 2)
                it = 0
                for mc in range(NDC):
                    for f in range(NF):
                        par = it % 2
                        if it >= 2:
                            te.wait_ge(s["pre3"], it - 1)
                        fs = slice(f * 512, (f + 1) * 512)
                        te.matmul(p_m3[par][:, :], wdt_bf[:, mc * P:(mc + 1) * P],
                                  dt0b_sb[:, fs], start=True, stop=False)
                        te.matmul(p_m3[par][:, :],
                                  wdt_bf[:, DH + mc * P:DH + (mc + 1) * P],
                                  dt1b_sb[:, fs], start=False, stop=True)
                        te.drain()
                        te.sem_inc(s["mm3"], 1)
                        it += 1

            @blk.vector
            def _(v):
                it = 0
                for mc in range(NDC):
                    for f in range(NF):
                        par = it % 2
                        v.wait_ge(s["mm3"], it + 1)
                        if it >= 2:
                            v.wait_ge(s["d3"], it - 1)
                        fs = slice(f * 512, (f + 1) * 512)
                        v.tensor_mul(dpre[par][:, :], p_m3[par][:, :],
                                     mskb_bc[:, fs]).then_inc(s["pre3"], 1)
                        v.drain()
                        it += 1

            @blk.scalar
            def _(sc):
                it = 0
                for mc in range(NDC):
                    for f in range(NF):
                        par = it % 2
                        sc.wait_ge(s["pre3"], it + 1)
                        if it >= 2:
                            sc.wait_ge(s["dst"], 16 * (it - 1))
                        sc.drain()
                        sc.activation(dexp[par][:, :], dpre[par][:, :], AF.Exp,
                                      bias=dt_bias_sb[:, mc:mc + 1])
                        sc.drain()
                        sc.activation(dout[par][:, :], dexp[par][:, :], AF.Ln,
                                      bias=1.0).then_inc(s["d3"], 1)
                        it += 1

            @blk.gpsimd
            def _(g):
                it = 0
                for mc in range(NDC):
                    for f in range(NF):
                        par = it % 2
                        g.wait_ge(s["d3"], it + 1)
                        g.wait_ge(s["dst"], 16 * it)
                        g.dma_start(
                            out=delta_sp.ap()[mc * P:(mc + 1) * P,
                                              f * 512:(f + 1) * 512],
                            in_=dout[par][:, :]).then_inc(s["dst"], 16)
                        it += 1
                g.wait_ge(s["dst"], 16 * NDC * NF)
            barrier()
        if stop_after == "B3":
            b_stack.close(); bc_stack.close()
            return nc
        b_stack.close()

        with contextlib.ExitStack() as ps:
            PS = ps.enter_context
            wo_stage = [PS(nc.sbuf_tensor(f"wo_stage{i}", [P, NKD * P // 2], F32)) for i in range(2)]
            wo_bfc = [PS(nc.sbuf_tensor(f"wo_bfc{i}", [P, NKD * P], BF16)) for i in range(2)]
            s = sems("W", ["ld", "c", "st"])
            WH = NKD * P // 2

            @blk.sync
            def _(sync):
                for mc in range(NKHH):
                    for hv in range(2):
                        j = 2 * mc + hv
                        if j >= 2:
                            sync.wait_ge(s["c"], j - 1)
                        sync.wait_ge(s["ld"], 16 * j)
                        sync.dma_start(
                            out=wo_stage[j % 2][:, :],
                            in_=w_out_t.ap()[mc][:, hv * WH:(hv + 1) * WH],
                        ).then_inc(s["ld"], 16)

            @blk.scalar
            def _(sc):
                for mc in range(NKHH):
                    for hv in range(2):
                        j = 2 * mc + hv
                        sc.wait_ge(s["ld"], 16 * (j + 1))
                        if mc >= 2 and hv == 0:
                            sc.wait_ge(s["st"], 16 * (mc - 1))
                        sc.copy(wo_bfc[mc % 2][:, hv * WH:(hv + 1) * WH],
                                wo_stage[j % 2][:, :]).then_inc(s["c"], 1)
                        sc.drain()

            @blk.gpsimd
            def _(g):
                for mc in range(NKHH):
                    g.wait_ge(s["c"], 2 * (mc + 1))
                    g.wait_ge(s["st"], 16 * mc)
                    g.dma_start(out=wob_sp.ap()[mc], in_=wo_bfc[mc % 2][:, :]) \
                        .then_inc(s["st"], 16)
                g.wait_ge(s["st"], 16 * NKHH)
            barrier()

        if stop_after == "Wc":
            bc_stack.close()
            return nc
        with contextlib.ExitStack() as ps:
            PS = ps.enter_context
            dA = [PS(nc.sbuf_tensor(f"dA{i}", [P, N * TC], F32)) for i in range(2)]
            bbig = [PS(nc.sbuf_tensor(f"bbig{i}", [P, N * TC], F32)) for i in range(2)]
            bbc = PS(nc.sbuf_tensor("bbc", [P, N * TC], BF16))
            cbc = PS(nc.sbuf_tensor("cbc", [P, N * TC], BF16))
            d_t = [PS(nc.sbuf_tensor(f"d_t{i}", [P, TC], F32)) for i in range(2)]
            u_t = [PS(nc.sbuf_tensor(f"u_t{i}", [P, TC], BF16)) for i in range(3)]
            g_t = [PS(nc.sbuf_tensor(f"g_t{i}", [P, TC], BF16)) for i in range(3)]
            du_t = PS(nc.sbuf_tensor("du_t", [P, TC], F32))
            i16 = PS(nc.sbuf_tensor("i16", [P, N], F32))
            yred = [PS(nc.sbuf_tensor(f"yred{i}", [P, TC], F32)) for i in range(2)]
            y2t = PS(nc.sbuf_tensor("y2t", [P, TC], F32))
            ybf = [PS(nc.sbuf_tensor(f"ybf{i}", [P, TC], BF16)) for i in range(2)]
            p_bc2 = [PS(nc.psum_tensor(f"p_bc2{i}", [P, 512], F32)) for i in range(2)]
            s = sems("C", ["dld", "ug", "dA", "scn", "cm",
                           "red", "assy", "gm", "yst", "bcmm", "bcev", "ag"])
            NTCr = C_NTC
            NIT = NTCr * NDC

            @blk.sync
            def _(sync):
                for it in range(NIT):
                    tc, dc = divmod(it, NDC)
                    rs = slice(dc * P, (dc + 1) * P)
                    cs = slice(tc * TC, (tc + 1) * TC)
                    if it >= 2:
                        sync.wait_ge(s["dA"], it - 1)
                        sync.wait_ge(s["scn"], it - 1)
                    sync.wait_ge(s["dld"], 16 * it)
                    sync.dma_start(out=d_t[it % 2][:, :],
                                   in_=delta_sp.ap()[rs, cs]).then_inc(s["dld"], 16)
                    if it >= 3:
                        sync.wait_ge(s["assy"], it - 2)
                        sync.wait_ge(s["gm"], it - 2)
                    sync.wait_ge(s["ug"], 16 * 2 * it)
                    sync.dma_start(out=u_t[it % 3][:, :],
                                   in_=ub_sp.ap()[rs, cs]).then_inc(s["ug"], 16)
                    sync.wait_ge(s["ug"], 16 * (2 * it + 1))
                    sync.dma_start(out=g_t[it % 3][:, :],
                                   in_=gate_sp.ap()[rs, cs]).then_inc(s["ug"], 16)
                    if it >= 2:
                        pit = it - 2
                        ptc, pdc = divmod(pit, NDC)
                        pcs = slice((ptc % 2) * TC, (ptc % 2 + 1) * TC)
                        sync.wait_ge(s["gm"], pit + 1)
                        sync.wait_ge(s["yst"], 16 * pit)
                        sync.dma_start(
                            out=yag_in[ptc // 2].ap()[pdc * P:(pdc + 1) * P, pcs],
                            in_=ybf[pit % 2][:, :]).then_inc(s["yst"], 16)
                for pit in (NIT - 2, NIT - 1):
                    ptc, pdc = divmod(pit, NDC)
                    pcs = slice((ptc % 2) * TC, (ptc % 2 + 1) * TC)
                    sync.wait_ge(s["gm"], pit + 1)
                    sync.wait_ge(s["yst"], 16 * pit)
                    sync.dma_start(out=yag_in[ptc // 2].ap()[pdc * P:(pdc + 1) * P, pcs],
                                   in_=ybf[pit % 2][:, :]).then_inc(s["yst"], 16)
                sync.wait_ge(s["yst"], 16 * NIT)

            @blk.tensor
            def _(te):
                for tc in range(NTCr):
                    if tc >= 1:
                        te.wait_ge(s["cm"], tc * NDC)
                    for n in range(N):
                        par = n % 2
                        base = tc * 2 * N + 2 * n
                        if base >= 2:
                            te.wait_ge(s["bcev"], base)
                        te.matmul(p_bc2[par][:, :],
                                  indrep_sb[:, n * P:(n + 1) * P],
                                  brow_sb[:, tc * TC:(tc + 1) * TC],
                                  start=True, stop=True)
                        te.drain()
                        te.sem_inc(s["bcmm"], 1)
                        if base + 1 >= 2:
                            te.wait_ge(s["bcev"], base)
                        te.matmul(p_bc2[1 - par][:, :],
                                  indrep_sb[:, n * P:(n + 1) * P],
                                  crow_sb[:, tc * TC:(tc + 1) * TC],
                                  start=True, stop=True)
                        te.drain()
                        te.sem_inc(s["bcmm"], 1)

            @blk.scalar
            def _(sc):
                for tc in range(NTCr):
                    for n in range(N):
                        par = n % 2
                        base = tc * 2 * N + 2 * n
                        sc.wait_ge(s["bcmm"], base + 1)
                        sc.copy(bbc[:, n * TC:(n + 1) * TC], p_bc2[par][:, :]) \
                            .then_inc(s["bcev"], 1)
                        sc.wait_ge(s["bcmm"], base + 2)
                        sc.copy(cbc[:, n * TC:(n + 1) * TC], p_bc2[1 - par][:, :]) \
                            .then_inc(s["bcev"], 1)
                        sc.drain()
                    for dc in range(NDC):
                        it = tc * NDC + dc
                        slot = it % 2
                        sc.wait_ge(s["dld"], 16 * (it + 1))
                        if it >= 2:
                            sc.wait_ge(s["red"], it - 1)
                        sc.drain()
                        if "dA" in C_SKIP:
                            for n in range(N):
                                ins = sc.copy(dA[slot][:, n * TC:(n + 1) * TC],
                                              d_t[it % 2][:, :])
                        else:
                            for n in range(N):
                                ins = sc.activation(
                                    dA[slot][:, n * TC:(n + 1) * TC],
                                    d_t[it % 2][:, :], AF.Exp,
                                    scale=a_sb[:, dc * N + n:dc * N + n + 1])
                        ins.then_inc(s["dA"], 1)

            @blk.vector
            def _(v):
                def reduce_step(j):
                    v.wait_ge(s["cm"], j + 1)
                    if "reduce" in C_SKIP:
                        v.tensor_copy(yred[j % 2][:, :], dA[j % 2][:, 0:TC]) \
                            .then_inc(s["red"], 1)
                    else:
                        v.tensor_reduce(
                            yred[j % 2][:, :],
                            dA[j % 2][:, :].rearrange("p (n t) -> p t n", n=N),
                            axis=AX.X, op=ALU.add).then_inc(s["red"], 1)
                    v.drain()

                def assy_step(pit):
                    pdc = pit % NDC
                    if pit >= 1:
                        v.wait_ge(s["gm"], pit)
                    if "assy" in C_SKIP:
                        v.tensor_copy(y2t[:, :], yred[pit % 2][:, :]) \
                            .then_inc(s["assy"], 1)
                        v.drain()
                        return
                    v.scalar_tensor_tensor(
                        y2t[:, :], u_t[pit % 3][:, :],
                        d_param_sb[:, pdc:pdc + 1], yred[pit % 2][:, :],
                        ALU.mult, ALU.add).then_inc(s["assy"], 1)
                    v.drain()

                for it in range(NIT):
                    tc, dc = divmod(it, NDC)
                    slot = it % 2
                    if it >= 2:
                        assy_step(it - 2)
                    dA3 = dA[slot][:, :].rearrange("p (n t) -> p n t", n=N)
                    bb = bbig[slot]
                    b3 = bb[:, :].rearrange("p (n t) -> p n t", n=N)
                    stt = state_sb[:, dc * N:(dc + 1) * N]
                    v.wait_ge(s["dld"], 16 * (it + 1))
                    v.wait_ge(s["ug"], 16 * (2 * it + 1))
                    v.tensor_mul(du_t[:, :], d_t[it % 2][:, :], u_t[it % 3][:, :])
                    v.drain()
                    if dc == 0:
                        v.wait_ge(s["bcev"], (tc + 1) * 2 * N)
                    if it >= 2:
                        v.wait_ge(s["cm"], it - 1)
                    if "bmul" in C_SKIP:
                        v.tensor_copy(bb[:, :], bbc[:, :])
                    else:
                        du_b = du_t[:, :].rearrange("p (o t) -> p o t", o=1) \
                            .broadcast_to([P, N, TC])
                        v.tensor_tensor(b3, du_b, bbc[:, :].rearrange(
                            "p (n t) -> p n t", n=N), ALU.mult)
                    v.drain()
                    v.wait_ge(s["dA"], it + 1)
                    if "inject" not in C_SKIP:
                        v.tensor_mul(i16[:, :], dA3[:, :, 0], stt)
                        v.drain()
                        v.tensor_add(b3[:, :, 0], b3[:, :, 0], i16[:, :])
                        v.drain()
                        v.memset(dA3[:, :, 0], 0.0)
                        v.drain()
                    if "scan" in C_SKIP:
                        v.tensor_copy(bb[:, :], dA[slot][:, :])
                    else:
                        v.tensor_tensor_scan(bb[:, :], dA[slot][:, :], bb[:, :],
                                             0.0, ALU.mult, ALU.add)
                    v.drain()
                    v.tensor_copy(stt, b3[:, :, TC - 1]).then_inc(s["scn"], 1)
                    v.drain()
                    if it >= 1:
                        reduce_step(it - 1)
                assy_step(NIT - 2)
                reduce_step(NIT - 1)
                assy_step(NIT - 1)

            @blk.gpsimd
            def _(g):
                def gate_step(pit):
                    g.wait_ge(s["assy"], pit + 1)
                    g.wait_ge(s["ug"], 16 * (2 * pit + 2))
                    if pit >= 2:
                        g.wait_ge(s["yst"], 16 * (pit - 1))
                    g.tensor_tensor(ybf[pit % 2][:, :], y2t[:, :],
                                    g_t[pit % 3][:, :], ALU.mult) \
                        .then_inc(s["gm"], 1)

                gnext = [0]

                def gate_upto(lim):
                    while gnext[0] < lim:
                        gate_step(gnext[0])
                        gnext[0] += 1

                for it in range(NIT):
                    slot = it % 2
                    if it >= 2:
                        gate_upto(it - 1)
                    g.wait_ge(s["scn"], it + 1)
                    if "cmul" in C_SKIP:
                        g.tensor_copy(dA[slot][:, :], bbig[slot][:, :]) \
                            .then_inc(s["cm"], 1)
                    else:
                        g.tensor_tensor(dA[slot][:, :], bbig[slot][:, :], cbc[:, :],
                                        ALU.mult).then_inc(s["cm"], 1)
                    tc, dc = divmod(it, NDC)
                    if dc == NDC - 1 and tc % 2 == 1:
                        gate_upto(it + 1)
                        g.wait_ge(s["yst"], 16 * NDC * (tc + 1))
                        if debug:
                            g.wait_ge(dbgg, 64 + 16 * (tc // 2))
                            g.dma_start(out=yag_dbg[tc // 2].ap(),
                                        in_=yag_in[tc // 2].ap()) \
                                .then_inc(dbgg, 16)
                        g.collective_compute(
                            "AllGather", ALU.bypass, replica_groups=PAIRS,
                            ins=[yag_in[tc // 2].ap().opt()],
                            outs=[yag_out[tc // 2].ap().opt()],
                        ).then_inc(s["ag"], 1)
                gate_upto(NIT)
                g.wait_ge(s["ag"], NTCr // 2)
                if debug:
                    g.wait_ge(dbgg, 64 + 16 * (NTCr // 2))
            barrier()
        bc_stack.close()

        if stop_after == "C":
            return nc
        with contextlib.ExitStack() as ps:
            PS = ps.enter_context
            y_slab = PS(nc.sbuf_tensor("y_slab", [P, NKD * 1024], BF16))
            wo_bf = [PS(nc.sbuf_tensor(f"wo_bfd{i}", [P, NKD * P], BF16)) for i in range(2)]
            xres_t = [PS(nc.sbuf_tensor(f"xres_t{i}", [P, 1024], F32)) for i in range(2)]
            r1_t = [PS(nc.sbuf_tensor(f"r1_t{i}", [P, 1024], F32)) for i in range(2)]
            sq_t = [PS(nc.sbuf_tensor(f"sq_td{i}", [P, 1024], F32)) for i in range(2)]
            p_m4 = [PS(nc.psum_tensor(f"p_m4{i}", [P, 1024], F32)) for i in range(2)]
            p_s1 = PS(nc.psum_tensor("p_s1", [1, 1024], F32))
            p_s2 = PS(nc.psum_tensor("p_s2", [1, 1024], F32))
            s1_t = PS(nc.sbuf_tensor("s1_t", [1, 1024], F32))
            s2_t = PS(nc.sbuf_tensor("s2_t", [1, 1024], F32))
            s = sems("D", ["yld", "wx", "mm4", "ev4", "sq", "smm",
                           "ss", "ssv", "r1st", "ar"])
            NSL = 4

            @blk.sync
            def _(sync):
                for sl in range(NSL):
                    for k in range(NKD):
                        if sl >= 1 and k == 0:
                            sync.wait_ge(s["mm4"], sl * NKHH)
                        sync.dma_start(
                            out=y_slab[:, k * 1024:(k + 1) * 1024],
                            in_=yag_out[sl].ap()[k * P:(k + 1) * P, :],
                        ).then_inc(s["yld"], 16)
                    for mc in range(NKHH):
                        i = sl * NKHH + mc
                        if i >= 2:
                            sync.wait_ge(s["mm4"], i - 1)
                        sync.wait_ge(s["wx"], 16 * 2 * i)
                        sync.dma_start(out=wo_bf[i % 2][:, :],
                                       in_=wob_sp.ap()[mc]).then_inc(s["wx"], 16)
                        if i >= 2:
                            sync.wait_ge(s["ev4"], i - 1)
                        sync.wait_ge(s["wx"], 16 * (2 * i + 1))
                        sync.dma_start(
                            out=xres_t[i % 2][:, :],
                            in_=x_res.ap()[mc * P:(mc + 1) * P,
                                           sl * 1024:(sl + 1) * 1024],
                        ).then_inc(s["wx"], 16)

            @blk.tensor
            def _(te):
                for sl in range(NSL):
                    te.wait_ge(s["yld"], 16 * NKD * (sl + 1))
                    for mc in range(NKHH):
                        i = sl * NKHH + mc
                        par = i % 2
                        te.wait_ge(s["wx"], 16 * (2 * i + 1))
                        if i >= 2:
                            te.wait_ge(s["ev4"], i - 1)
                        for k in range(NKD):
                            for f in range(2):
                                ins = te.matmul(
                                    p_m4[par][:, f * 512:(f + 1) * 512],
                                    wo_bf[par][:, k * P:(k + 1) * P],
                                    y_slab[:, k * 1024 + f * 512:
                                           k * 1024 + (f + 1) * 512],
                                    start=(k == 0), stop=(k == NKD - 1))
                        te.drain()
                        te.sem_inc(s["mm4"], 1)
                        if i >= 1:
                            j = i - 1
                            te.wait_ge(s["sq"], j + 1)
                            for f in range(2):
                                fs = slice(f * 512, (f + 1) * 512)
                                te.matmul(p_s1[:, fs], ones_c_sb[:, :],
                                          r1_t[j % 2][:, fs],
                                          start=(j % NKHH == 0),
                                          stop=(j % NKHH == NKHH - 1))
                                te.matmul(p_s2[:, fs], ones_c_sb[:, :],
                                          sq_t[j % 2][:, fs],
                                          start=(j % NKHH == 0),
                                          stop=(j % NKHH == NKHH - 1))
                            te.drain()
                            te.sem_inc(s["smm"], 1)
                j = NSL * NKHH - 1
                te.wait_ge(s["sq"], j + 1)
                for f in range(2):
                    fs = slice(f * 512, (f + 1) * 512)
                    te.matmul(p_s1[:, fs], ones_c_sb[:, :], r1_t[j % 2][:, fs],
                              start=(j % NKHH == 0), stop=True)
                    te.matmul(p_s2[:, fs], ones_c_sb[:, :], sq_t[j % 2][:, fs],
                              start=(j % NKHH == 0), stop=True)
                te.drain()
                te.sem_inc(s["smm"], 1)

            @blk.vector
            def _(v):
                for i in range(NSL * NKHH):
                    par = i % 2
                    v.wait_ge(s["mm4"], i + 1)
                    v.wait_ge(s["wx"], 16 * (2 * i + 2))
                    if i >= 2:
                        v.wait_ge(s["r1st"], 16 * (i - 1))
                        v.wait_ge(s["smm"], i - 1)
                    v.tensor_add(r1_t[par][:, :], p_m4[par][:, :], xres_t[par][:, :]) \
                        .then_inc(s["ev4"], 1)
                    v.drain()
                    v.tensor_mul(sq_t[par][:, :], r1_t[par][:, :], r1_t[par][:, :]) \
                        .then_inc(s["sq"], 1)
                    v.drain()
                    if i % NKHH == NKHH - 1:
                        sl = i // NKHH
                        v.wait_ge(s["smm"], i + 1)
                        if sl >= 1:
                            v.wait_ge(s["ss"], 32 * sl)
                        v.tensor_copy(s1_t[:, :], p_s1[:, :])
                        v.tensor_copy(s2_t[:, :], p_s2[:, :]).then_inc(s["ssv"], 1)
                        v.drain()

            @blk.gpsimd
            def _(g):
                for i in range(NSL * NKHH):
                    sl, mc = divmod(i, NKHH)
                    g.wait_ge(s["ev4"], i + 1)
                    g.wait_ge(s["r1st"], 16 * i)
                    g.dma_start(
                        out=r1_sp.ap()[mc * P:(mc + 1) * P, sl * 1024:(sl + 1) * 1024],
                        in_=r1_t[i % 2][:, :]).then_inc(s["r1st"], 16)
                    if mc == NKHH - 1:
                        g.wait_ge(s["ssv"], sl + 1)
                        g.wait_ge(s["ss"], 32 * sl)
                        g.dma_start(out=st1_in.ap()[0:1, sl * 1024:(sl + 1) * 1024],
                                    in_=s1_t[:, :]).then_inc(s["ss"], 16)
                        g.dma_start(out=st1_in.ap()[1:2, sl * 1024:(sl + 1) * 1024],
                                    in_=s2_t[:, :]).then_inc(s["ss"], 16)
                g.wait_ge(s["r1st"], 16 * NSL * NKHH)
                g.wait_ge(s["ss"], 32 * NSL)
                g.collective_compute(
                    "AllReduce", ALU.add, replica_groups=PAIRS,
                    ins=[st1_in.ap().opt()], outs=[st1_out.ap().opt()],
                ).then_inc(s["ar"], 1)
                g.wait_ge(s["ar"], 1)
            barrier()

        if stop_after == "D1":
            return nc
        x2_stack = contextlib.ExitStack()
        x2b_sb = x2_stack.enter_context(nc.sbuf_tensor("x2b_sb", [P, NKHH * L], BF16))

        with contextlib.ExitStack() as ps:
            PS = ps.enter_context
            s1_row = PS(nc.sbuf_tensor("s1b_row", [1, L], F32))
            s2_row = PS(nc.sbuf_tensor("s2b_row", [1, L], F32))
            tmp512 = PS(nc.sbuf_tensor("tmp512", [1, 512], F32))
            p_bcd = [PS(nc.psum_tensor(f"p_bcd{i}", [P, 512], F32)) for i in range(2)]
            r1d_t = [PS(nc.sbuf_tensor(f"r1d_t{i}", [P, 512], F32)) for i in range(2)]
            x2_t = [PS(nc.sbuf_tensor(f"x2d_t{i}", [P, 512], F32)) for i in range(2)]
            s = sems("E", ["ld", "vi", "bcmm", "rld", "x2", "x2c", "x2st"])

            @blk.sync
            def _(sync):
                sync.dma_start(out=s1_row[:, :], in_=st1_out.ap()[0:1, :]).then_inc(s["ld"], 16)
                sync.dma_start(out=s2_row[:, :], in_=st1_out.ap()[1:2, :]).then_inc(s["ld"], 16)

            @blk.vector
            def _(v):
                v.wait_ge(s["ld"], 32)
                v.tensor_scalar_mul(s1_row[:, :], s1_row[:, :], 1.0 / H)
                v.drain()
                for f in range(NF):
                    fs = slice(f * 512, (f + 1) * 512)
                    v.tensor_mul(tmp512[:, :], s1_row[:, fs], s1_row[:, fs])
                    v.drain()
                    ins = v.scalar_tensor_tensor(s2_row[:, fs], s2_row[:, fs],
                                                 1.0 / H, tmp512[:, :],
                                                 ALU.mult, ALU.subtract)
                    if f == NF - 1:
                        ins.then_inc(s["vi"], 1)
                    v.drain()

            @blk.scalar
            def _(sc):
                sc.wait_ge(s["vi"], 1)
                sc.activation(s2_row[:, :], s2_row[:, :], AF.Ln, bias=eps_sb[0:1, 1:2])
                sc.drain()
                sc.activation(s2_row[:, :], s2_row[:, :], AF.Exp, scale=-0.5) \
                    .then_inc(s["vi"], 1)
                sc.drain()

            @blk.tensor
            def _(te):
                te.wait_ge(s["vi"], 2)
                for f in range(NF):
                    if f >= 1:
                        te.wait_ge(s["x2"], NKHH * f)
                    fs = slice(f * 512, (f + 1) * 512)
                    te.matmul(p_bcd[0][:, :], ones_r_sb[:, :], s1_row[:, fs],
                              start=True, stop=True)
                    te.matmul(p_bcd[1][:, :], ones_r_sb[:, :], s2_row[:, fs],
                              start=True, stop=True)
                    te.drain()
                    te.sem_inc(s["bcmm"], 1)

            @blk.sync
            def _(sync):
                i = 0
                for f in range(NF):
                    for mc in range(NKHH):
                        if i >= 2:
                            sync.wait_ge(s["x2"], i - 1)
                        sync.wait_ge(s["rld"], 16 * i)
                        sync.dma_start(
                            out=r1d_t[i % 2][:, :],
                            in_=r1_sp.ap()[mc * P:(mc + 1) * P, f * 512:(f + 1) * 512],
                        ).then_inc(s["rld"], 16)
                        i += 1

            @blk.vector
            def _(v):
                i = 0
                for f in range(NF):
                    v.wait_ge(s["bcmm"], f + 1)
                    for mc in range(NKHH):
                        par = i % 2
                        v.wait_ge(s["rld"], 16 * (i + 1))
                        if i >= 2:
                            v.wait_ge(s["x2c"], i - 1)
                            v.wait_ge(s["x2st"], 16 * (i - 1))
                        v.tensor_sub(x2_t[par][:, :], r1d_t[par][:, :],
                                     p_bcd[0][:, :])
                        v.drain()
                        v.tensor_mul(x2_t[par][:, :], x2_t[par][:, :],
                                     p_bcd[1][:, :]).then_inc(s["x2"], 1)
                        v.drain()
                        i += 1

            @blk.scalar
            def _(sc):
                i = 0
                for f in range(NF):
                    for mc in range(NKHH):
                        sc.wait_ge(s["x2"], i + 1)
                        sc.copy(x2b_sb[:, mc * L + f * 512:mc * L + (f + 1) * 512],
                                x2_t[i % 2][:, :]).then_inc(s["x2c"], 1)
                        sc.drain()
                        i += 1

            @blk.gpsimd
            def _(g):
                i = 0
                for f in range(NF):
                    for mc in range(NKHH):
                        g.wait_ge(s["x2"], i + 1)
                        g.wait_ge(s["x2st"], 16 * i)
                        g.dma_start(
                            out=x2_sp.ap()[mc * P:(mc + 1) * P, f * 512:(f + 1) * 512],
                            in_=x2_t[i % 2][:, :]).then_inc(s["x2st"], 16)
                        i += 1
                g.wait_ge(s["x2st"], 16 * NKHH * NF)
            barrier()

        if stop_after == "D2":
            x2_stack.close()
            return nc
        e_stack = contextlib.ExitStack()
        EE = e_stack.enter_context
        bfl_bf = EE(nc.sbuf_tensor("bfl_bf", [P, HH], BF16))
        lows_sb = EE(nc.sbuf_tensor("lows_sb", [P, L], BF16))

        with contextlib.ExitStack() as ps:
            PS = ps.enter_context
            s = sems("F", ["ld", "wc", "mmr", "evr", "shp", "ar",
                           "exp", "esmm", "esev", "rec", "repmm", "lows", "lld"])
            wr_bf = PS(nc.sbuf_tensor("wr_bf", [P, NKHH * 8], BF16))
            af_bf = PS(nc.sbuf_tensor("af_bf", [P, NKHH * P], BF16))
            with contextlib.ExitStack() as stg:
                SS = stg.enter_context
                wr_stage = SS(nc.sbuf_tensor("wr_stage", [P, NKHH * 8], F32))
                af_stage = SS(nc.sbuf_tensor("af_stage", [P, NKHH * P], F32))
                bfl_stage = SS(nc.sbuf_tensor("bfl_stage", [P, HH], F32))

                @blk.sync
                def _(sync):
                    sync.dma_start(out=wr_stage[:, :], in_=wr_t.ap()).then_inc(s["ld"], 16)
                    sync.dma_start(out=af_stage[:, :], in_=a_flat_t.ap()).then_inc(s["ld"], 16)
                    sync.dma_start(out=bfl_stage[:, :], in_=b_flat_t.ap()).then_inc(s["ld"], 16)

                @blk.scalar
                def _(sc):
                    sc.wait_ge(s["ld"], 48)
                    sc.copy(wr_bf[:, :], wr_stage[:, :])
                    sc.copy(af_bf[:, :], af_stage[:, :])
                    sc.activation(bfl_bf[:, :], bfl_stage[:, :], AF.Copy, scale=2.0) \
                        .then_inc(s["wc"], 1)

            p_r = [PS(nc.psum_tensor(f"p_r{i}", [8, 512], F32)) for i in range(2)]
            p_low = [PS(nc.psum_tensor(f"p_low{i}", [P, 512], F32)) for i in range(2)]
            t_r = [PS(nc.sbuf_tensor(f"t_r{i}", [8, 512], F32)) for i in range(2)]
            t_low = [PS(nc.sbuf_tensor(f"t_low{i}", [P, 512], F32)) for i in range(2)]
            e_sb = PS(nc.sbuf_tensor("e_sb", [8, L], F32))
            p_es = PS(nc.psum_tensor("p_es", [1, 512], F32))
            es_row = PS(nc.sbuf_tensor("es_row", [1, L], F32))
            rc_row = es_row
            t_er = PS(nc.sbuf_tensor("t_er", [P, 512], F32))
            rlow_t = [PS(nc.sbuf_tensor(f"rlow_t{i}", [P, 512], F32)) for i in range(2)]

            @blk.tensor
            def _(te):
                te.wait_ge(s["wc"], 1)
                for f in range(NF):
                    par = f % 2
                    if f >= 2:
                        te.wait_ge(s["evr"], f - 1)
                    for k in range(NKHH):
                        rhs = x2b_sb[:, k * L + f * 512:k * L + (f + 1) * 512]
                        st, sp = (k == 0), (k == NKHH - 1)
                        te.matmul(p_r[par][:, :], wr_bf[:, k * 8:(k + 1) * 8], rhs,
                                  start=st, stop=sp)
                        te.matmul(p_low[par][:, :],
                                  af_bf[:, k * P:(k + 1) * P], rhs,
                                  start=st, stop=sp)
                    te.drain()
                    te.sem_inc(s["mmr"], 1)

            @blk.vector
            def _(v):
                for f in range(NF):
                    par = f % 2
                    v.wait_ge(s["mmr"], f + 1)
                    if f >= 2:
                        v.wait_ge(s["shp"], 32 * (f - 1))
                    v.tensor_copy(t_r[par][:, :], p_r[par][:, :])
                    v.tensor_copy(t_low[par][:, :], p_low[par][:, :]).then_inc(s["evr"], 1)
                v.drain()

            @blk.gpsimd
            def _(g):
                for f in range(NF):
                    par = f % 2
                    fs = slice(f * 512, (f + 1) * 512)
                    g.wait_ge(s["evr"], f + 1)
                    g.dma_start(out=rtr_in.ap()[:, fs], in_=t_r[par][:, :]) \
                        .then_inc(s["shp"], 16)
                    g.dma_start(out=low_in.ap()[:, fs], in_=t_low[par][:, :]) \
                        .then_inc(s["shp"], 16)
                g.wait_ge(s["shp"], 32 * NF)
                g.collective_compute(
                    "AllReduce", ALU.add, replica_groups=PAIRS,
                    ins=[rtr_in.ap().opt()], outs=[rtr_out.ap().opt()],
                ).then_inc(s["ar"], 1)
                g.collective_compute(
                    "AllReduce", ALU.add, replica_groups=PAIRS,
                    ins=[low_in.ap().opt()], outs=[low_out.ap().opt()],
                ).then_inc(s["ar"], 1)
                if debug:
                    g.wait_ge(s["ar"], 2)
                    g.dma_start(out=low_dbg.ap(), in_=low_out.ap()) \
                        .then_inc(dbgg, 16)
                    g.wait_ge(dbgg, 64 + 16 * (NTC // 2) + 16)

            @blk.sync
            def _(sync):
                sync.wait_ge(s["ar"], 2)
                sync.dma_start(out=e_sb[:, :], in_=rtr_out.ap()).then_inc(s["ld"], 16)

            @blk.scalar
            def _(sc):
                sc.wait_ge(s["ld"], 64)
                sc.activation(e_sb[:, :], e_sb[:, :], AF.Exp, bias=br_sb[:, :]) \
                    .then_inc(s["exp"], 1)
                sc.drain()

            @blk.tensor
            def _(te):
                te.wait_ge(s["exp"], 1)
                for f in range(NF):
                    if f >= 1:
                        te.wait_ge(s["esev"], f)
                    fs = slice(f * 512, (f + 1) * 512)
                    te.matmul(p_es[:, :], ones16_sb[0:8, :], e_sb[:, fs],
                              start=True, stop=True)
                    te.drain()
                    te.sem_inc(s["esmm"], 1)

            @blk.vector
            def _(v):
                for f in range(NF):
                    fs = slice(f * 512, (f + 1) * 512)
                    v.wait_ge(s["esmm"], f + 1)
                    v.tensor_copy(es_row[:, fs], p_es[:, :]).then_inc(s["esev"], 1)
                v.drain()
                v.reciprocal(rc_row[:, :], es_row[:, :]).then_inc(s["rec"], 1)
                v.drain()

            p_rc = [PS(nc.psum_tensor(f"p_rc{i}", [P, 512], F32)) for i in range(2)]

            @blk.tensor
            def _(te):
                te.wait_ge(s["rec"], 1)
                for f in range(NF):
                    par = f % 2
                    if f >= 2:
                        te.wait_ge(s["lows"], f - 1)
                    fs = slice(f * 512, (f + 1) * 512)
                    te.matmul(p_low[par][:, :], rrep_sb[:, :], e_sb[:, fs],
                              start=True, stop=True)
                    te.matmul(p_rc[par][:, :], ones_r_sb[:, :], rc_row[:, fs],
                              start=True, stop=True)
                    te.drain()
                    te.sem_inc(s["repmm"], 1)

            @blk.sync
            def _(sync):
                i = 0
                for f in range(NF):
                    par = f % 2
                    fs = slice(f * 512, (f + 1) * 512)
                    if f >= 2:
                        sync.wait_ge(s["lows"], f - 1)
                    sync.wait_ge(s["lld"], 16 * f)
                    sync.dma_start(out=rlow_t[par][:, :], in_=low_out.ap()[:, fs]) \
                        .then_inc(s["lld"], 16)
                    i += 1

            @blk.vector
            def _(v):
                for f in range(NF):
                    par = f % 2
                    fs = slice(f * 512, (f + 1) * 512)
                    v.wait_ge(s["repmm"], f + 1)
                    v.tensor_copy(t_er[:, :], p_low[par][:, :])
                    v.drain()
                    v.tensor_mul(t_er[:, :], t_er[:, :], p_rc[par][:, :])
                    v.drain()
                    v.wait_ge(s["lld"], 16 * (f + 1))
                    v.tensor_mul(lows_sb[:, fs], rlow_t[par][:, :], t_er[:, :]) \
                        .then_inc(s["lows"], 1)
                    v.drain()
            barrier()

        if stop_after == "E1":
            e_stack.close(); x2_stack.close()
            return nc
        with contextlib.ExitStack() as ps2:
            PS2 = ps2.enter_context
            x2e_t = [PS2(nc.sbuf_tensor(f"x2e_t{i}", [P, 512], F32)) for i in range(2)]
            r2_t = [PS2(nc.sbuf_tensor(f"r2_t{i}", [P, 512], F32)) for i in range(2)]
            sq_t = [PS2(nc.sbuf_tensor(f"sq_te{i}", [P, 512], F32)) for i in range(2)]
            p_f2 = [PS2(nc.psum_tensor(f"p_f2{i}", [P, 512], F32)) for i in range(2)]
            p_s1e = PS2(nc.psum_tensor("p_s1e", [1, 512], F32))
            p_s2e = PS2(nc.psum_tensor("p_s2e", [1, 512], F32))
            s1e_t = PS2(nc.sbuf_tensor("s1e_t", [1, 512], F32))
            s2e_t = PS2(nc.sbuf_tensor("s2e_t", [1, 512], F32))
            se = sems("G", ["xld", "mmf", "ev", "sq", "smm", "ss",
                            "ssv", "r2st", "ar"])

            @blk.sync
            def _(sync):
                i = 0
                for f in range(NF):
                    for mc in range(NKHH):
                        if i >= 2:
                            sync.wait_ge(se["ev"], i - 1)
                        sync.wait_ge(se["xld"], 16 * i)
                        sync.dma_start(
                            out=x2e_t[i % 2][:, :],
                            in_=x2_sp.ap()[mc * P:(mc + 1) * P,
                                           f * 512:(f + 1) * 512],
                        ).then_inc(se["xld"], 16)
                        i += 1

            @blk.tensor
            def _(te):
                i = 0
                for f in range(NF):
                    for mc in range(NKHH):
                        par = i % 2
                        if i >= 2:
                            te.wait_ge(se["ev"], i - 1)
                        te.matmul(p_f2[par][:, :],
                                  bfl_bf[:, mc * P:(mc + 1) * P],
                                  lows_sb[:, f * 512:(f + 1) * 512],
                                  start=True, stop=True)
                        te.drain()
                        te.sem_inc(se["mmf"], 1)
                        if i >= 1:
                            j = i - 1
                            te.wait_ge(se["sq"], j + 1)
                            te.matmul(p_s1e[:, :], ones_c_sb[:, :],
                                      r2_t[j % 2][:, :],
                                      start=(j % NKHH == 0),
                                      stop=(j % NKHH == NKHH - 1))
                            te.matmul(p_s2e[:, :], ones_c_sb[:, :],
                                      sq_t[j % 2][:, :],
                                      start=(j % NKHH == 0),
                                      stop=(j % NKHH == NKHH - 1))
                            te.drain()
                            te.sem_inc(se["smm"], 1)
                        i += 1
                j = NF * NKHH - 1
                te.wait_ge(se["sq"], j + 1)
                te.matmul(p_s1e[:, :], ones_c_sb[:, :], r2_t[j % 2][:, :],
                          start=(j % NKHH == 0), stop=True)
                te.matmul(p_s2e[:, :], ones_c_sb[:, :], sq_t[j % 2][:, :],
                          start=(j % NKHH == 0), stop=True)
                te.drain()
                te.sem_inc(se["smm"], 1)

            @blk.vector
            def _(v):
                i = 0
                for f in range(NF):
                    for mc in range(NKHH):
                        par = i % 2
                        v.wait_ge(se["mmf"], i + 1)
                        v.wait_ge(se["xld"], 16 * (i + 1))
                        if i >= 2:
                            v.wait_ge(se["r2st"], 16 * (i - 1))
                            v.wait_ge(se["smm"], i - 1)
                        v.tensor_add(r2_t[par][:, :], p_f2[par][:, :],
                                     x2e_t[par][:, :]).then_inc(se["ev"], 1)
                        v.drain()
                        v.tensor_mul(sq_t[par][:, :], r2_t[par][:, :],
                                     r2_t[par][:, :]).then_inc(se["sq"], 1)
                        v.drain()
                        if mc == NKHH - 1:
                            v.wait_ge(se["smm"], i + 1)
                            if f >= 1:
                                v.wait_ge(se["ss"], 32 * f)
                            v.tensor_copy(s1e_t[:, :], p_s1e[:, :])
                            v.tensor_copy(s2e_t[:, :], p_s2e[:, :]).then_inc(se["ssv"], 1)
                            v.drain()
                        i += 1

            @blk.gpsimd
            def _(g):
                i = 0
                for f in range(NF):
                    for mc in range(NKHH):
                        g.wait_ge(se["ev"], i + 1)
                        g.wait_ge(se["r2st"], 16 * i)
                        g.dma_start(
                            out=r2_sp.ap()[mc * P:(mc + 1) * P,
                                           f * 512:(f + 1) * 512],
                            in_=r2_t[i % 2][:, :]).then_inc(se["r2st"], 16)
                        if mc == NKHH - 1:
                            g.wait_ge(se["ssv"], f + 1)
                            g.wait_ge(se["ss"], 32 * f)
                            g.dma_start(out=st2_in.ap()[0:1, f * 512:(f + 1) * 512],
                                        in_=s1e_t[:, :]).then_inc(se["ss"], 16)
                            g.dma_start(out=st2_in.ap()[1:2, f * 512:(f + 1) * 512],
                                        in_=s2e_t[:, :]).then_inc(se["ss"], 16)
                        i += 1
                g.wait_ge(se["r2st"], 16 * NF * NKHH)
                g.wait_ge(se["ss"], 32 * NF)
                g.collective_compute(
                    "AllReduce", ALU.add, replica_groups=PAIRS,
                    ins=[st2_in.ap().opt()], outs=[st2_out.ap().opt()],
                ).then_inc(se["ar"], 1)
                g.wait_ge(se["ar"], 1)
            barrier()
        if stop_after == "E2":
            e_stack.close(); x2_stack.close()
            return nc
        e_stack.close()
        x2_stack.close()

        with contextlib.ExitStack() as ps:
            PS = ps.enter_context
            s1_row = PS(nc.sbuf_tensor("s1f_row", [1, L], F32))
            s2_row = PS(nc.sbuf_tensor("s2f_row", [1, L], F32))
            tmp512 = PS(nc.sbuf_tensor("tmp512f", [1, 512], F32))
            p_bcf = [PS(nc.psum_tensor(f"p_bcf{i}", [P, 512], F32)) for i in range(2)]
            r2f_t = [PS(nc.sbuf_tensor(f"r2f_t{i}", [P, 512], F32)) for i in range(2)]
            o_t = [PS(nc.sbuf_tensor(f"o_t{i}", [P, 512], F32)) for i in range(2)]
            s = sems("Hh", ["ld", "vi", "bcmm", "rld", "o", "ost"])

            @blk.sync
            def _(sync):
                sync.dma_start(out=s1_row[:, :], in_=st2_out.ap()[0:1, :]).then_inc(s["ld"], 16)
                sync.dma_start(out=s2_row[:, :], in_=st2_out.ap()[1:2, :]).then_inc(s["ld"], 16)

            @blk.vector
            def _(v):
                v.wait_ge(s["ld"], 32)
                v.tensor_scalar_mul(s1_row[:, :], s1_row[:, :], 1.0 / H)
                v.drain()
                for f in range(NF):
                    fs = slice(f * 512, (f + 1) * 512)
                    v.tensor_mul(tmp512[:, :], s1_row[:, fs], s1_row[:, fs])
                    v.drain()
                    ins = v.scalar_tensor_tensor(s2_row[:, fs], s2_row[:, fs],
                                                 1.0 / H, tmp512[:, :],
                                                 ALU.mult, ALU.subtract)
                    if f == NF - 1:
                        ins.then_inc(s["vi"], 1)
                    v.drain()

            @blk.scalar
            def _(sc):
                sc.wait_ge(s["vi"], 1)
                sc.activation(s2_row[:, :], s2_row[:, :], AF.Ln, bias=eps_sb[0:1, 1:2])
                sc.drain()
                sc.activation(s2_row[:, :], s2_row[:, :], AF.Exp, scale=-0.5) \
                    .then_inc(s["vi"], 1)
                sc.drain()

            @blk.tensor
            def _(te):
                te.wait_ge(s["vi"], 2)
                for f in range(NF):
                    if f >= 1:
                        te.wait_ge(s["o"], NKHH * f)
                    fs = slice(f * 512, (f + 1) * 512)
                    te.matmul(p_bcf[0][:, :], ones_r_sb[:, :], s1_row[:, fs],
                              start=True, stop=True)
                    te.matmul(p_bcf[1][:, :], ones_r_sb[:, :], s2_row[:, fs],
                              start=True, stop=True)
                    te.drain()
                    te.sem_inc(s["bcmm"], 1)

            @blk.sync
            def _(sync):
                i = 0
                for f in range(NF):
                    for mc in range(NKHH):
                        if i >= 2:
                            sync.wait_ge(s["o"], i - 1)
                        sync.wait_ge(s["rld"], 16 * i)
                        sync.dma_start(
                            out=r2f_t[i % 2][:, :],
                            in_=r2_sp.ap()[mc * P:(mc + 1) * P, f * 512:(f + 1) * 512],
                        ).then_inc(s["rld"], 16)
                        i += 1

            @blk.vector
            def _(v):
                i = 0
                for f in range(NF):
                    v.wait_ge(s["bcmm"], f + 1)
                    for mc in range(NKHH):
                        par = i % 2
                        v.wait_ge(s["rld"], 16 * (i + 1))
                        if i >= 2:
                            v.wait_ge(s["ost"], 16 * (i - 1))
                        v.tensor_sub(o_t[par][:, :], r2f_t[par][:, :],
                                     p_bcf[0][:, :])
                        v.drain()
                        v.tensor_mul(o_t[par][:, :], o_t[par][:, :],
                                     p_bcf[1][:, :]).then_inc(s["o"], 1)
                        v.drain()
                        i += 1

            @blk.gpsimd
            def _(g):
                i = 0
                for f in range(NF):
                    for mc in range(NKHH):
                        g.wait_ge(s["o"], i + 1)
                        g.wait_ge(s["ost"], 16 * i)
                        g.dma_start(
                            out=out_ht.ap()[mc * P:(mc + 1) * P, f * 512:(f + 1) * 512],
                            in_=o_t[i % 2][:, :]).then_inc(s["ost"], 16)
                        i += 1
                g.wait_ge(s["ost"], 16 * NKHH * NF)
            barrier()

    return nc


def _prep_inputs(inputs):
    hs = np.ascontiguousarray(np.asarray(inputs["L0_attn_hidden_states"], dtype=np.float32))
    W_in = np.asarray(inputs["L0_attn_W_in"], dtype=np.float32)
    conv_w = np.asarray(inputs["L0_attn_conv_w"], dtype=np.float32)
    conv_b = np.asarray(inputs["L0_attn_conv_b"], dtype=np.float32)
    Wx = np.asarray(inputs["L0_attn_Wx"], dtype=np.float32)
    dt_ln = np.asarray(inputs["L0_attn_dt_ln_w"], dtype=np.float32)
    b_ln = np.asarray(inputs["L0_attn_b_ln_w"], dtype=np.float32)
    c_ln = np.asarray(inputs["L0_attn_c_ln_w"], dtype=np.float32)
    Wdt = np.asarray(inputs["L0_attn_Wdt"], dtype=np.float32)
    dt_bias = np.asarray(inputs["L0_attn_dt_bias"], dtype=np.float32)
    A = np.asarray(inputs["L0_attn_A"], dtype=np.float32)
    D_param = np.asarray(inputs["L0_attn_D_param"], dtype=np.float32)
    W_out = np.asarray(inputs["L0_attn_W_out"], dtype=np.float32)
    fA = np.asarray(inputs["L0_ffn_A"], dtype=np.float32)
    fB = np.asarray(inputs["L0_ffn_B"], dtype=np.float32)
    fWr = np.asarray(inputs["L0_ffn_Wr"], dtype=np.float32)
    fbr = np.asarray(inputs["L0_ffn_br"], dtype=np.float32)

    def chunk_cols(v, nch):
        return np.ascontiguousarray(v.reshape(nch, P).T)

    ones_r = np.ones((1, P), np.float32)
    ones_c = np.ones((P, 1), np.float32)
    ones16 = np.ones((16, 1), np.float32)
    rrep = np.zeros((8, P), np.float32)
    for h in range(8):
        rrep[h, h * 16:(h + 1) * 16] = 8.0
    indrep = np.zeros((16, 16 * P), ml_dtypes.bfloat16)
    for n in range(16):
        indrep[n, n * P:(n + 1) * P] = 1.0

    A_flatT = fA.reshape(P, H).T
    B_flatT = np.transpose(fB, (0, 2, 1)).reshape(P, H)
    WrT = fWr.T
    W_outT = W_out.T
    WxT = Wx.T
    WdtT = Wdt.T

    in_maps = []
    for c in range(8):
        b = c // 2
        hf = c % 2
        ds = slice(hf * DH, (hf + 1) * DH)
        hsl = slice(hf * HH, (hf + 1) * HH)
        xTc = np.ascontiguousarray(hs[b].T)
        Wcat = np.concatenate([W_in[ds], W_in[D + hf * DH: D + (hf + 1) * DH]], 0)
        w_in_tiles = np.ascontiguousarray(
            Wcat.T.reshape(NKH, P, 2 * NDC, P).transpose(2, 1, 0, 3)
        ).reshape(2 * NDC, P, NKH * P)
        wx_c = np.ascontiguousarray(
            WxT[ds].reshape(NKH, P, 288).transpose(1, 0, 2)).reshape(P, NKH * 288)
        wdt_c = np.ascontiguousarray(
            WdtT[:, ds].reshape(2, P, DH).transpose(1, 0, 2)).reshape(P, 2 * DH)
        w_out_tiles = np.ascontiguousarray(
            W_outT[:, hsl].reshape(NKD, P, NKHH, P).transpose(2, 1, 0, 3)
        ).reshape(NKHH, P, NKD * P)
        a_flat_c = np.ascontiguousarray(
            A_flatT[hsl].reshape(NKHH, P, P).transpose(1, 0, 2)).reshape(P, NKHH * P)
        wr_c = np.ascontiguousarray(
            WrT[hsl].reshape(NKHH, P, 8).transpose(1, 0, 2)).reshape(P, NKHH * 8)

        m = {
            "xT": xTc,
            "x_res": np.ascontiguousarray(xTc[hsl]),
            "w_in_t": w_in_tiles,
            "conv_w_t": np.ascontiguousarray(
                conv_w[ds].reshape(NDC, P, KC).transpose(1, 0, 2)).reshape(P, NDC * KC),
            "conv_b_t": chunk_cols(conv_b[ds], NDC),
            "wx_t": wx_c,
            "wdt_t": wdt_c,
            "dt_ln_t": chunk_cols(dt_ln, 2),
            "b_ln_t": np.ascontiguousarray(b_ln.reshape(16, 1)),
            "c_ln_t": np.ascontiguousarray(c_ln.reshape(16, 1)),
            "dt_bias_t": chunk_cols(dt_bias[ds], NDC),
            "a_t": np.ascontiguousarray(
                A[ds].reshape(NDC, P, N).transpose(1, 0, 2)).reshape(P, NDC * N),
            "d_param_t": chunk_cols(D_param[ds], NDC),
            "w_out_t": w_out_tiles,
            "a_flat_t": a_flat_c,
            "b_flat_t": np.ascontiguousarray(B_flatT[:, hsl]),
            "wr_t": wr_c,
            "br_t": np.ascontiguousarray(fbr.reshape(8, 1)),
            "rrep_t": rrep,
            "indrep_t": indrep,
            "ones_r": ones_r,
            "ones_c": ones_c,
            "ones16": ones16,
            "eps_t": np.array([[EPS_RMS, EPS_LN]], np.float32),
        }
        in_maps.append(m)
    return in_maps


def kernel(**inputs):
    in_maps = _prep_inputs(inputs)
    nc = build_nc(debug=False)
    res = run_bass_kernel_spmd(nc, in_maps, core_ids=list(range(8)))
    out = np.empty((B_, L, H), np.float32)
    for c in range(8):
        b, hf = c // 2, c % 2
        out[b, :, hf * HH:(hf + 1) * HH] = np.asarray(res.results[c]["out_ht"]).T
    return out
